# version 12
# baseline (speedup 1.0000x reference)
from contextlib import ExitStack

import numpy as np
import concourse.bass as bass
import concourse.mybir as mybir
from concourse.bass_utils import run_bass_kernel_spmd

F32 = mybir.dt.float32
BF16 = mybir.dt.bfloat16
I32 = mybir.dt.int32
ALU = mybir.AluOpType
ACT = mybir.ActivationFunctionType
AX = mybir.AxisListType

NCORES = 8
DM = 1024
DI = 2048
DH = 4096
NST = 16
TOK = 2048
PRE = 320
POST = 64
TSLAB = PRE + TOK + POST
NTT = TSLAB // 128
XIN_OFF = PRE - 259
XIN_LEN = 3 + 256 + TOK
SC_OFF = PRE - 256
SC_LEN = 256 + TOK
OWN_SC = 256
FF_OFF = PRE - 2
FF_LEN = TOK + 4
EPS = 1e-5

SC_CHUNKS = [(0, 512), (512, 512), (1024, 512), (1536, 512), (2048, 256)]
XIN_CHUNKS = [(0, 512), (512, 512), (1024, 512), (1536, 512), (2048, 259)]
OWN_CHUNKS = [(0, 512), (512, 512), (1024, 512), (1536, 512)]
FF_CHUNKS = [(0, 512), (512, 512), (1024, 512), (1536, 512), (2048, 4)]

ENGS = ("sync", "tensor", "vector", "scalar", "gpsimd")
ARW = 28672


_NO_TRACK = {"wait_ge", "sem_inc", "sem_clear", "wait_op", "drain"}


def _ap_span(ap):
    big = (ap.tensor.name, 0, 1 << 40, 0, 1 << 40)
    try:
        dims = [(int(s), int(c)) for s, c in ap.ap]
        off = int(ap.offset)
    except (TypeError, ValueError, AttributeError):
        return big
    if not dims or any(s < 0 for s, c in dims):
        return big
    ps, pc = dims[0]
    if ps > 0:
        p_lo, c_lo = off // ps, off % ps
    else:
        p_lo, c_lo = 0, off
    c_end = c_lo + sum(s * (c - 1) for s, c in dims[1:]) + 1
    if c_end > ps > 0:
        return big
    sz = mybir.dt.size(ap.dtype)
    return (ap.tensor.name, p_lo, p_lo + pc, c_lo * sz, c_end * sz)


def _overlap(a, b):
    return (a[0] == b[0] and a[1] < b[2] and b[1] < a[2]
            and a[3] < b[4] and b[3] < a[4])


class _EngProxy:

    def __init__(self, eng, ctx):
        object.__setattr__(self, "_e", eng)
        object.__setattr__(self, "_c", ctx)

    def _flush(self):
        c = self._c
        if c.last is None:
            c.dirty = []
            return
        if c.last_sem is None:
            c.cnt[c.auto_sem] += 1
            c.last.then_inc(c.sems[c.auto_sem], 1)
            c.last_sem = c.auto_sem
        self._e.wait_ge(c.sems[c.last_sem], c.cnt[c.last_sem])
        c.dirty = []

    def __getattr__(self, name):
        f = getattr(self._e, name)
        if not callable(f):
            return f
        if name in _NO_TRACK:
            return f

        def g(*a, **k):
            c = self._c
            pos = [x for x in a if isinstance(x, bass.AP)]
            outs, ins_ = [], []
            for key in ("out", "accum_out"):
                v = k.get(key)
                if isinstance(v, bass.AP):
                    outs.append(v)
            if "out" not in k and pos:
                outs.append(pos[0])
                ins_ = pos[1:]
            else:
                ins_ = list(pos)
            for key in ("bias", "scale", "alpha", "initial", "scalar",
                        "scalar1", "scalar2", "in_", "in0", "in1", "mask",
                        "data", "identity"):
                v = k.get(key)
                if isinstance(v, bass.AP):
                    ins_.append(v)
            if name == "memset":
                ins_ = []
            if name.endswith("dma_start"):
                spans = [_ap_span(x) for x in ins_]
                if any(_overlap(s, d) for s in spans for d in c.dirty):
                    self._flush()
                return f(*a, **k)
            if not c.nochk:
                spans = [_ap_span(x) for x in ins_]
                if any(_overlap(s, d) for s in spans for d in c.dirty):
                    self._flush()
            r = f(*a, **k)
            if hasattr(r, "then_inc"):
                c.last = r
                c.last_sem = None
                for o in outs:
                    c.dirty.append(_ap_span(o))
            return r
        return g


class Ctx:

    def __init__(self, eng_name, eng, sems, sb, ps, dram, ins, out):
        self.en = eng_name
        self.raw = eng
        self.last = None
        self.last_sem = None
        self.dirty = []
        self.nochk = False
        self.auto_sem = "au_" + eng_name
        self.eng = _EngProxy(eng, self)
        self.sems = sems
        self.cnt = {k: 0 for k in sems}
        self.nbar = 0
        self.sb = sb
        self.ps = ps
        self.dram = dram
        self.ins = ins
        self.out = out

    def on(self, name):
        return self.en == name

    def ar(self, c0, w, dt=F32):
        a = self.sb["AR"]
        if dt == F32:
            return a[:, c0:c0 + w]
        if dt == BF16:
            return a[:, c0:c0 + (w + 1) // 2].bitcast(BF16)[:, 0:w]
        if dt == I32:
            return a[:, c0:c0 + w].bitcast(I32)
        raise ValueError(dt)

    def dma(self, qeng, out, in_, sem):
        self.cnt[sem] += 16
        if self.en == qeng:
            s = _ap_span(in_)
            if any(_overlap(s, d) for d in self.dirty):
                self.eng._flush()
            i = self.raw.dma_start(out=out, in_=in_)
            i.then_inc(self.sems[sem], 16)

    def flush(self, owner):
        if self.en == owner:
            self.eng._flush()

    def sig(self, owner, sem):
        self.cnt[sem] += 1
        if self.en == owner:
            if self.last is not None and self.last_sem is None:
                self.last.then_inc(self.sems[sem], 1)
                self.last_sem = sem
            elif self.last is not None:
                self.raw.wait_ge(self.sems[self.last_sem],
                                 self.cnt[self.last_sem])
                self.raw.sem_inc(self.sems[sem], 1)
                self.last = None
                self.dirty = []
                self.last_sem = None
            else:
                self.raw.sem_inc(self.sems[sem], 1)

    def ss(self, owner, sem):
        self.sig(owner, sem)
        self.wt(owner, sem)

    def wt(self, waiter, sem, lag=0):
        v = self.cnt[sem] - lag
        if self.en == waiter and v > 0:
            self.raw.wait_ge(self.sems[sem], v)

    def wt_abs(self, waiter, sem, v):
        if self.en == waiter and v > 0:
            self.raw.wait_ge(self.sems[sem], v)

    def barrier(self):
        if self.en == "sync":
            self.wt("sync", "ld")
            self.last = None
        if self.en == "gpsimd":
            self.wt("gpsimd", "gd")
        self.nbar += 1
        if self.last is not None and self.last_sem is None:
            self.last.then_inc(self.sems["bar"], 1)
        elif self.last is not None:
            self.raw.wait_ge(self.sems[self.last_sem], self.cnt[self.last_sem])
            self.raw.sem_inc(self.sems["bar"], 1)
        else:
            self.raw.sem_inc(self.sems["bar"], 1)
        self.last = None
        self.last_sem = None
        self.dirty = []
        self.raw.wait_ge(self.sems["bar"], len(ENGS) * self.nbar)


def ph_consts(c: Ctx):
    sb, ps, ins = c.sb, c.ps, c.ins
    loads = [
        (sb["alog"][:], ins["alog"]),
        (sb["cvw"][:], ins["cvw"]),
        (sb["cvb"][:], ins["cvb"]),
        (sb["dtb"][:], ins["dtb"]),
        (sb["fcw"][:], ins["fcw"]),
        (sb["fcb"][:], ins["fcb"]),
        (sb["iden"][:], ins["iden"]),
        (sb["idenb"][:], ins["idenb"]),
        (sb["onesr"][:], ins["onesr"]),
        (sb["onescol"][:], ins["onescol"]),
        (sb["msk"][:], ins["msk"]),
        (sb["kc"][:], ins["kconst"]),
        (sb["wbc"][:], ins["w_bcT"]),
    ]
    for dst, src in loads:
        c.dma("sync", dst, src, "ld")
    if c.on("scalar"):
        c.wt("scalar", "ld")
        c.eng.activation(sb["A"][:], sb["alog"][:], ACT.Exp, bias=0.0, scale=1.0)
        c.eng.activation(sb["A"][:], sb["A"][:], ACT.Copy, bias=0.0, scale=-1.0)
    c.sig("scalar", "sA")

    if c.on("vector"):
        c.wt("vector", "ld")
        c.eng.memset(sb["acc"][:, 0:2], 0.0)
    c.sig("vector", "vA")
    for wi, (wname, ntile, width) in enumerate([("w1T", 8, DH), ("w2T", 32, DM)]):
        for j in range(ntile):
            slot = j % 2
            wm = c.ar(slot * 8192 // 2, DH)
            c.wt("sync", "vB", lag=1)
            c.dma("sync", wm[:, 0:width], ins[wname][j * 128:(j + 1) * 128, :],
                  "ld")
            if c.on("vector"):
                c.wt("vector", "ld")
                c.eng.tensor_reduce(sb["acc"][:, 2 + wi:3 + wi], wm[:, 0:width],
                                    AX.X, ALU.add, apply_absolute_value=True)
                c.eng.tensor_add(sb["acc"][:, wi:wi + 1], sb["acc"][:, wi:wi + 1],
                                 sb["acc"][:, 2 + wi:3 + wi])
            c.sig("vector", "vB")
    c.sig("vector", "vC")
    if c.on("tensor"):
        c.wt("tensor", "vC")
        c.eng.matmul(ps[7][0:1, 0:1], sb["acc"][:, 0:1], sb["onescol"][:],
                     start=True, stop=True)
        c.eng.matmul(ps[7][0:1, 1:2], sb["acc"][:, 1:2], sb["onescol"][:],
                     start=True, stop=True)
    c.sig("tensor", "tA")
    if c.on("vector"):
        c.wt("vector", "tA")
        c.eng.tensor_scalar(sb["sw"][0:1, 0:2], ps[7][0:1, 0:2],
                            1.0 / (DH * DM), EPS, ALU.mult, ALU.max)
    if c.on("vector"):
        c.eng.reciprocal(sb["sw"][0:1, 2:4], sb["sw"][0:1, 0:2])
    c.sig("vector", "sB")
    if c.on("tensor"):
        c.wt("tensor", "sB")
        c.eng.matmul(ps[7][:, 4:8], sb["onesr"][0:1, :], sb["sw"][0:1, 0:4],
                     start=True, stop=True)
    c.sig("tensor", "tB")
    if c.on("vector"):
        c.wt("vector", "tB")
        c.eng.tensor_copy(sb["swc"][:, 0:4], ps[7][:, 4:8])
    c.sig("vector", "vE")
    c.barrier()


def ph_ln(c: Ctx):
    sb, ps, ins = c.sb, c.ps, c.ins
    xqf = c.ar(18944, 8 * TSLAB, BF16)
    for j in range(NTT):
        slot = j % 2
        xt = sb["row"][:, slot * 1024:(slot + 1) * 1024]
        xn = c.ar(6144 + slot * 1024, DM)
        c.wt("sync", "vF", lag=1)
        c.dma("sync", xt, ins["xs"][j * 128:(j + 1) * 128, :], "ld")
        if c.on("vector"):
            c.wt("vector", "ld")
            c.eng.tensor_reduce(sb["tmp"][:, 0:1], xt, AX.X, ALU.add)
            c.eng.tensor_scalar_mul(sb["tmp"][:, 0:1], sb["tmp"][:, 0:1],
                                    1.0 / DM)
            c.eng.tensor_scalar(xn, xt, sb["tmp"][:, 0:1], None, ALU.subtract,
                                ALU.bypass)
        c.sig("vector", "vF")
        c.sig("vector", "vG")
        if c.on("scalar"):
            c.wt("scalar", "vG")
            c.eng.activation(c.ar(3072, DM), xn, ACT.Square, bias=0.0, scale=1.0,
                             accum_out=sb["tmp"][:, 1:2])
            c.ss("scalar", "ps")
            c.eng.activation(sb["tmp"][:, 2:3], sb["tmp"][:, 1:2], ACT.Sqrt,
                             bias=EPS, scale=1.0 / DM)
        c.sig("scalar", "sC")
        if c.on("vector"):
            c.wt("vector", "sC")
            c.eng.reciprocal(sb["tmp"][:, 6:7], sb["tmp"][:, 2:3])
            c.ss("vector", "pv")
            c.eng.tensor_scalar(xn, xn, sb["tmp"][:, 6:7], None, ALU.mult,
                                ALU.bypass)
            c.eng.tensor_reduce(sb["tmp"][:, 3:4], xn, AX.X, ALU.max,
                                apply_absolute_value=True)
            c.eng.tensor_scalar_max(sb["tmp"][:, 3:4], sb["tmp"][:, 3:4], EPS)
            c.eng.reciprocal(sb["tmp"][:, 4:5], sb["tmp"][:, 3:4])
            c.eng.tensor_scalar_mul(sb["tmp"][:, 5:6], sb["tmp"][:, 4:5], 127.0)
            c.eng.reciprocal(sb["sti"][:, j:j + 1], sb["tmp"][:, 5:6])
            v = c.ar(4096, DM)
            c.ss("vector", "pv")
            c.eng.tensor_scalar(v, xn, sb["tmp"][:, 5:6], None, ALU.mult,
                                ALU.bypass)
            k0 = sb["ki"][:, 0:DM]
            c.eng.tensor_copy(k0, v)
            d = c.ar(5120, DM)
            c.eng.tensor_tensor(d, v, k0, ALU.subtract)
            p_ = c.ar(2048, DM)
            c.eng.tensor_scalar(p_, d, 0.5, None, ALU.is_gt, ALU.bypass)
            c.eng.scalar_tensor_tensor(d, d, -0.5, p_, ALU.is_lt, ALU.subtract)
            xqt = c.ar(0, DM, BF16)
            c.eng.tensor_tensor(xqt, k0, d, ALU.subtract)
        c.sig("vector", "vH")
        if c.on("tensor"):
            c.wt("tensor", "vH")
            c.wt("tensor", "vI")
            c.wt("tensor", "sD")
            for f in range(8):
                bank = ps[f // 4]
                c.eng.transpose(bank[:, (f % 4) * 128:(f % 4) * 128 + 128],
                                xn[:, f * 128:(f + 1) * 128], sb["iden"][:])
            pb = ps[2][:].bitcast(BF16)
            for f in range(8):
                c.eng.transpose(pb[:, f * 128:(f + 1) * 128],
                                c.ar(0, DM, BF16)[:, f * 128:(f + 1) * 128],
                                sb["idenb"][:])
        c.sig("tensor", "tC")
        c.sig("tensor", "tD")
        if c.on("scalar"):
            c.wt("scalar", "tD")
            pb = ps[2][:].bitcast(BF16)
            for f in range(8):
                c.eng.copy(xqf[:, f * TSLAB + j * 128: f * TSLAB + (j + 1) * 128],
                           pb[:, f * 128:(f + 1) * 128])
        c.sig("scalar", "sD")
        if c.on("vector"):
            c.wt("vector", "tC")
            c.wt("vector", "gd")
            for f in range(8):
                bank = ps[f // 4]
                c.eng.tensor_copy(c.ar(1024, 1024)[:, f * 128:(f + 1) * 128],
                                  bank[:, (f % 4) * 128:(f % 4) * 128 + 128])
        c.sig("vector", "vI")
        for h in range(2):
            c.wt("gpsimd", "vI")
            c.dma("gpsimd",
                  c.dram["xnfm"][h * 512:(h + 1) * 512,
                                 j * 128:(j + 1) * 128].rearrange(
                      "(a p) q -> p a q", p=128),
                  c.ar(1024, 1024)[:, h * 512:(h + 1) * 512].rearrange(
                      "p (a q) -> p a q", a=4), "gd")
    c.barrier()


def ph_ffn1(c: Ctx):
    sb, ps, ins = c.sb, c.ps, c.ins
    srep = c.ar(0, 2052)
    stg = [c.ar(2052, 512), c.ar(2564, 512)]
    g01 = c.ar(3076, 1024)
    if c.on("tensor"):
        c.eng.transpose(ps[0][0:NTT, 0:128], sb["sti"][:, 0:NTT], sb["iden"][:])
    c.sig("tensor", "tE")
    if c.on("vector"):
        c.wt("vector", "tE")
        c.eng.tensor_copy(sb["sqt"][0:NTT, 0:128], ps[0][0:NTT, 0:128])
    c.sig("vector", "vJ0")
    c.wt("sync", "vJ0")
    c.dma("sync", sb["srow"][0:1, :], sb["sqt"][0:NTT, 0:128], "ld")
    if c.on("vector"):
        c.wt("vector", "ld")
        c.eng.tensor_scalar(sb["srow"][0:1, :], sb["srow"][0:1, :],
                            sb["sw"][0:1, 0:1], None, ALU.mult, ALU.bypass)
    c.sig("vector", "vJ")
    for ci, (c0, L) in enumerate(FF_CHUNKS):
        bank = ps[4 + ci % 2]
        if c.on("tensor"):
            c.wt("tensor", "vJ")
            c.wt("tensor", "vK", lag=1)
            c.eng.matmul(bank[:, 0:L], sb["onesr"][0:1, :],
                         sb["srow"][0:1, FF_OFF + c0:FF_OFF + c0 + L],
                         start=True, stop=True)
        c.sig("tensor", "tF")
        if c.on("vector"):
            c.wt("vector", "tF")
            c.eng.tensor_copy(srep[:, c0:c0 + L], bank[:, 0:L])
        c.sig("vector", "vK")

    for m in range(32):
        slot = m % 2
        wsl = sb["wsl"][:, slot * 1024:(slot + 1) * 1024]
        c.wt("sync", "sE", lag=1)
        c.dma("sync", wsl.rearrange("p (k m) -> p k m", k=8),
              ins["w1T"][:, m * 128:(m + 1) * 128].rearrange(
                  "(k p) m -> p k m", p=128), "ld")
        if c.on("scalar"):
            c.wt("scalar", "ld")
            c.wt("scalar", "vL")
            c.eng.activation(sb["sqt"][:, 0:1024], wsl, ACT.Abs, bias=0.0,
                             scale=sb["swc"][:, 2:3])
            c.eng.sign(sb["sqt2"][:, 0:1024], wsl)
        c.sig("scalar", "sE")
        if c.on("vector"):
            c.wt("vector", "sE")
            c.eng.tensor_scalar(g01, sb["sqt"][:, 0:1024], 0.5, None, ALU.is_gt,
                                ALU.bypass)
            c.eng.tensor_tensor(sb["wq"][:, slot * 1024:(slot + 1) * 1024],
                                g01, sb["sqt2"][:, 0:1024], ALU.mult)
        c.sig("vector", "vL")
        for ci, (c0, L) in enumerate(FF_CHUNKS):
            bank = ps[(m * len(FF_CHUNKS) + ci) % 2]
            wq = sb["wq"][:, slot * 1024:(slot + 1) * 1024]
            if c.on("tensor"):
                c.wt("tensor", "vL")
                c.wt("tensor", "vM", lag=1)
                for k in range(8):
                    c.eng.matmul(bank[:, 0:L], wq[:, k * 128:(k + 1) * 128],
                                 xqf_slice(c, k, FF_OFF + c0, L),
                                 start=(k == 0), stop=(k == 7))
            c.sig("tensor", "tG")
            if c.on("vector"):
                c.wt("vector", "tG")
                c.wt("vector", "gd", lag=16)
                c.eng.tensor_tensor(stg[ci % 2][:, 0:L], bank[:, 0:L],
                                    srep[:, c0:c0 + L], ALU.mult)
            c.sig("vector", "vM")
            c.wt("gpsimd", "vM")
            c.dma("gpsimd", c.dram["h1"][m * 128:(m + 1) * 128, c0:c0 + L],
                  stg[ci % 2][:, 0:L], "gd")
    c.barrier()


def xqf_slice(c, k, c0, L):
    return c.ar(18944, 8 * TSLAB, BF16)[:, k * TSLAB + c0:k * TSLAB + c0 + L]


def ph_ffn2(c: Ctx):
    sb = c.sb
    hmax = c.ar(4096, 2048)
    sil = c.ar(6144, 2048)
    for g in range(32):
        row = sb["row"][:, 0:FF_LEN]
        c.wt("sync", "vN")
        c.dma("sync", row, c.dram["h1"][g * 128:(g + 1) * 128, :], "ld")
        if c.on("vector"):
            c.wt("vector", "ld")
            c.wt("vector", "sF2")
            acc = sb["sqt"][:, 0:TOK]
            acc2 = sb["sqt2"][:, 0:TOK]
            c.eng.tensor_scalar(acc, row[:, 1:1 + TOK], sb["fcw"][:, g * 3:g * 3 + 1],
                                None, ALU.mult, ALU.bypass)
            c.eng.scalar_tensor_tensor(acc2, row[:, 2:2 + TOK],
                                       sb["fcw"][:, g * 3 + 1:g * 3 + 2], acc,
                                       ALU.mult, ALU.add)
            c.eng.scalar_tensor_tensor(acc, row[:, 3:3 + TOK],
                                       sb["fcw"][:, g * 3 + 2:g * 3 + 3], acc2,
                                       ALU.mult, ALU.add)
        c.sig("vector", "vN")
        if c.on("scalar"):
            c.wt("scalar", "vN")
            c.wt("scalar", "vO")
            c.wt("scalar", "gd")
            c.eng.activation(sil, sb["sqt"][:, 0:TOK], ACT.Silu,
                             bias=sb["fcb"][:, g:g + 1], scale=1.0)
        c.sig("scalar", "sF")
        if c.on("scalar"):
            c.eng.activation(sb["sqt2"][:, 0:TOK], sil, ACT.Abs, bias=0.0,
                             scale=1.0)
        c.sig("scalar", "sF2")
        if c.on("vector"):
            c.wt("vector", "sF2")
            if g == 0:
                c.eng.tensor_copy(hmax, sb["sqt2"][:, 0:TOK])
            else:
                c.eng.tensor_max(hmax, hmax, sb["sqt2"][:, 0:TOK])
        c.sig("vector", "vO")
        c.wt("gpsimd", "vO")
        c.dma("gpsimd", c.dram["h1"][g * 128:(g + 1) * 128, 0:TOK], sil, "gd")
    c.barrier()


def ph_ffn3(c: Ctx):
    sb, ps, ins = c.sb, c.ps, c.ins
    hmax = c.ar(4096, 2048)
    srep2 = c.ar(2048, 2048)
    p_ = c.ar(0, 2048)
    for k in range(16):
        bank = ps[k % 2]
        if c.on("tensor"):
            c.wt("tensor", "vP", lag=1)
            c.eng.transpose(bank[:, 0:128], hmax[:, k * 128:(k + 1) * 128],
                            sb["iden"][:])
        c.sig("tensor", "tH")
        if c.on("vector"):
            c.wt("vector", "tH")
            c.eng.tensor_reduce(sb["q2"][:, k:k + 1], bank[:, 0:128], AX.X,
                                ALU.max)
        c.sig("vector", "vP")
    if c.on("vector"):
        c.eng.tensor_scalar_max(sb["q2"][:, 0:16], sb["q2"][:, 0:16], EPS)
        c.eng.reciprocal(sb["q2"][:, 16:32], sb["q2"][:, 0:16])
        c.eng.tensor_scalar_mul(sb["q2"][:, 32:48], sb["q2"][:, 16:32], 127.0)
        c.eng.reciprocal(sb["q2"][:, 48:64], sb["q2"][:, 32:48])
        c.eng.tensor_scalar(sb["q2"][:, 64:80], sb["q2"][:, 48:64],
                            sb["swc"][:, 1:2], None, ALU.mult, ALU.bypass)
    c.sig("vector", "vR")
    if c.on("tensor"):
        c.wt("tensor", "vR")
        c.eng.transpose(ps[0][0:16, 0:128], sb["q2"][:, 32:48], sb["iden"][:])
    c.sig("tensor", "tI")
    if c.on("vector"):
        c.wt("vector", "tI")
        c.eng.tensor_copy(sb["sqt"][0:16, 0:128], ps[0][0:16, 0:128])
    c.sig("vector", "vR0")
    c.wt("sync", "vR0")
    c.dma("sync", sb["srow"][0:1, 0:TOK], sb["sqt"][0:16, 0:128], "ld")
    for ci, (c0, L) in enumerate(OWN_CHUNKS):
        bank = ps[ci % 2]
        if c.on("tensor"):
            c.wt("tensor", "ld")
            c.wt("tensor", "vS", lag=1)
            c.eng.matmul(bank[:, 0:L], sb["onesr"][0:1, :],
                         sb["srow"][0:1, c0:c0 + L], start=True, stop=True)
        c.sig("tensor", "tJ")
        if c.on("vector"):
            c.wt("vector", "tJ")
            c.eng.tensor_copy(srep2[:, c0:c0 + L], bank[:, 0:L])
        c.sig("vector", "vS")

    for g in range(32):
        row = sb["row"][:, 0:TOK]
        c.wt("sync", "vT", lag=0)
        c.dma("sync", row, c.dram["h1"][g * 128:(g + 1) * 128, 0:TOK], "ld")
        if c.on("vector"):
            c.wt("vector", "ld")
            c.wt("vector", "gd")
            v = sb["sqt"][:, 0:TOK]
            c.eng.tensor_tensor(v, row, srep2, ALU.mult)
            k0 = sb["ki"][:, 0:TOK]
            c.eng.tensor_copy(k0, v)
            d = sb["sqt2"][:, 0:TOK]
            c.eng.tensor_tensor(d, v, k0, ALU.subtract)
            c.eng.tensor_scalar(p_, d, 0.5, None, ALU.is_gt, ALU.bypass)
            c.eng.scalar_tensor_tensor(d, d, -0.5, p_, ALU.is_lt, ALU.subtract)
            c.eng.tensor_tensor(sb["wq"][:, 0:2048], k0, d, ALU.subtract)
        c.sig("vector", "vT")
        c.wt("gpsimd", "vT")
        c.dma("gpsimd", c.dram["hq2"][g * 128:(g + 1) * 128, :],
              sb["wq"][:, 0:2048], "gd")
    for k in range(32):
        slot = k % 2
        wsl = sb["wsl"][:, slot * 1024:(slot + 1) * 1024]
        c.wt("sync", "sH", lag=1)
        c.dma("sync", wsl, ins["w2T"][k * 128:(k + 1) * 128, :], "ld")
        if c.on("scalar"):
            c.wt("scalar", "ld")
            c.wt("scalar", "vU")
            c.wt("scalar", "vT")
            c.eng.activation(sb["sqt"][:, 0:1024], wsl, ACT.Abs, bias=0.0,
                             scale=sb["swc"][:, 3:4])
            c.eng.sign(sb["sqt2"][:, 0:1024], wsl)
        c.sig("scalar", "sH")
        if c.on("vector"):
            c.wt("vector", "sH")
            c.eng.tensor_scalar(p_[:, 0:1024], sb["sqt"][:, 0:1024], 0.5, None,
                                ALU.is_gt, ALU.bypass)
            c.eng.tensor_tensor(c.ar(6144, 32768, BF16)[:, k * 1024:(k + 1) * 1024],
                                p_[:, 0:1024], sb["sqt2"][:, 0:1024], ALU.mult)
        c.sig("vector", "vU")
    c.barrier()


def ph_ffn4(c: Ctx):
    sb, ps, ins = c.sb, c.ps, c.ins
    w2q = c.ar(6144, 32768, BF16)
    lhs = c.ar(0, 4096, BF16)
    stg = [c.ar(2048, 512), c.ar(2560, 512)]
    for tt in range(16):
        xrow = sb["row"][:, 0:1024]
        c.wt("sync", "vV", lag=0)
        c.dma("sync", xrow, ins["xs"][PRE + tt * 128:PRE + (tt + 1) * 128, :],
              "ld")
        for k in range(32):
            c.dma("sync", lhs[:, k * 128:(k + 1) * 128],
                  c.dram["hq2"][k * 128:(k + 1) * 128,
                                tt * 128:(tt + 1) * 128], "ld")
        for oc in range(2):
            bank = ps[oc]
            if c.on("tensor"):
                c.wt("tensor", "ld")
                c.wt("tensor", "vV", lag=1)
                for k in range(32):
                    c.eng.matmul(bank[:, 0:512], lhs[:, k * 128:(k + 1) * 128],
                                 w2q[:, k * 1024 + oc * 512:
                                     k * 1024 + oc * 512 + 512],
                                 start=(k == 0), stop=(k == 31))
            c.sig("tensor", "tK")
            if c.on("vector"):
                c.wt("vector", "tK")
                c.wt("vector", "gd", lag=16)
                c.eng.scalar_tensor_tensor(stg[oc][:, 0:512], bank[:, 0:512],
                                           sb["q2"][:, 64 + tt:65 + tt],
                                           xrow[:, oc * 512:(oc + 1) * 512],
                                           ALU.mult, ALU.add)
            c.sig("vector", "vV")
            c.wt("gpsimd", "vV")
            c.dma("gpsimd", c.dram["fout"][tt * 128:(tt + 1) * 128,
                                           oc * 512:(oc + 1) * 512],
                  stg[oc][:, 0:512], "gd")
    c.barrier()


def ph_win(c: Ctx):
    sb, ps, ins = c.sb, c.ps, c.ins
    bigx = c.ar(0, 19456)
    mrep = c.ar(19456, 2304)
    stg = [c.ar(21760, 512), c.ar(22272, 512)]
    for f in range(8):
        c.dma("sync", bigx[:, f * TSLAB:(f + 1) * TSLAB],
              c.dram["xnfm"][f * 128:(f + 1) * 128, :], "ld")
    for ci, (c0, L) in enumerate(SC_CHUNKS):
        bank = ps[2 + ci % 2]
        if c.on("tensor"):
            c.wt("tensor", "ld")
            c.wt("tensor", "vW", lag=1)
            c.eng.matmul(bank[:, 0:L], sb["onesr"][0:1, :],
                         sb["msk"][0:1, c0:c0 + L], start=True, stop=True)
        c.sig("tensor", "tL")
        if c.on("vector"):
            c.wt("vector", "tL")
            c.eng.tensor_copy(mrep[:, c0:c0 + L], bank[:, 0:L])
        c.sig("vector", "vW")

    for m in range(32):
        slot = m % 2
        wsl = sb["wsl"][:, slot * 1024:(slot + 1) * 1024]
        c.wt("sync", "tM", lag=1)
        c.dma("sync", wsl.rearrange("p (k m) -> p k m", k=8),
              ins["w_inT"][:, m * 128:(m + 1) * 128].rearrange(
                  "(k p) m -> p k m", p=128), "ld")
        chunks = XIN_CHUNKS if m < 16 else OWN_CHUNKS
        base = XIN_OFF if m < 16 else PRE
        for ci, (c0, L) in enumerate(chunks):
            bank = ps[ci % 2]
            if c.on("tensor"):
                c.wt("tensor", "ld")
                c.wt("tensor", "eT", lag=1)
                for k in range(8):
                    c.eng.matmul(bank[:, 0:L], wsl[:, k * 128:(k + 1) * 128],
                                 bigx[:, k * TSLAB + base + c0:
                                      k * TSLAB + base + c0 + L],
                                 start=(k == 0), stop=(k == 7))
            c.sig("tensor", "tN")
            if m < 16:
                if c.on("vector"):
                    c.wt("vector", "tN")
                    c.eng.tensor_copy(sb["row"][:, c0:c0 + L], bank[:, 0:L])
                c.sig("vector", "eT")
            else:
                if c.on("scalar"):
                    c.wt("scalar", "tN")
                    c.wt("scalar", "gd", lag=16)
                    c.eng.activation(stg[ci % 2][:, 0:L], bank[:, 0:L], ACT.Silu,
                                     bias=0.0, scale=1.0)
                c.sig("scalar", "eT")
                c.sig("scalar", "sI")
                c.wt("gpsimd", "sI")
                c.dma("gpsimd",
                      c.dram["sz"][(m - 16) * 128:(m - 15) * 128, c0:c0 + L],
                      stg[ci % 2][:, 0:L], "gd")
        if m < 16:
            for ci, (c0, L) in enumerate(SC_CHUNKS):
                if c.on("vector"):
                    c.wt("vector", "sI")
                    acc = sb["sqt"][:, 0:512]
                    acc2 = sb["sqt2"][:, 0:512]
                    row = sb["row"]
                    c.eng.tensor_scalar(acc[:, 0:L], row[:, c0:c0 + L],
                                        sb["cvw"][:, m * 4:m * 4 + 1], None,
                                        ALU.mult, ALU.bypass)
                    c.eng.scalar_tensor_tensor(
                        acc2[:, 0:L], row[:, c0 + 1:c0 + 1 + L],
                        sb["cvw"][:, m * 4 + 1:m * 4 + 2], acc[:, 0:L],
                        ALU.mult, ALU.add)
                    c.eng.scalar_tensor_tensor(
                        acc[:, 0:L], row[:, c0 + 2:c0 + 2 + L],
                        sb["cvw"][:, m * 4 + 2:m * 4 + 3], acc2[:, 0:L],
                        ALU.mult, ALU.add)
                    c.eng.scalar_tensor_tensor(
                        acc2[:, 0:L], row[:, c0 + 3:c0 + 3 + L],
                        sb["cvw"][:, m * 4 + 3:m * 4 + 4], acc[:, 0:L],
                        ALU.mult, ALU.add)
                    c.eng.tensor_tensor(acc[:, 0:L], acc2[:, 0:L],
                                        mrep[:, c0:c0 + L], ALU.mult)
                c.sig("vector", "vX")
                if c.on("scalar"):
                    c.wt("scalar", "vX")
                    c.wt("scalar", "gd", lag=16)
                    c.eng.activation(stg[ci % 2][:, 0:L], sb["sqt"][:, 0:L],
                                     ACT.Silu, bias=sb["cvb"][:, m:m + 1],
                                     scale=1.0)
                c.sig("scalar", "sI")
                c.wt("gpsimd", "sI")
                c.dma("gpsimd",
                      c.dram["xconv"][m * 128:(m + 1) * 128, c0:c0 + L],
                      stg[ci % 2][:, 0:L], "gd")
            c.sig("tensor", "tM")
        else:
            c.sig("tensor", "tM")
    c.barrier()


def dt_tiles_load(c: Ctx, c0, L):
    for k in range(16):
        c.wt("sync", "vY")
        c.dma("sync", c.ar(16384 + k * 512, 512)[:, 0:L],
              c.dram["xconv"][k * 128:(k + 1) * 128, c0:c0 + L], "ld")


def dt_m_slice(c: Ctx, c0, L, m):
    sb, ps, ins = c.sb, c.ps, c.ins
    stg = [c.ar(24576, 512), c.ar(25088, 512)]
    slot = m % 2
    wsl = sb["wsl"][:, slot * 2048:slot * 2048 + 2048]
    c.wt("sync", "tO", lag=1)
    c.dma("sync", wsl.rearrange("p (k m) -> p k m", k=16),
          ins["w_dtT"][:, m * 128:(m + 1) * 128].rearrange(
              "(k p) m -> p k m", p=128), "ld")
    bank = ps[m % 2]
    if c.on("tensor"):
        c.wt("tensor", "ld")
        c.wt("tensor", "eU", lag=1)
        for k in range(16):
            c.eng.matmul(bank[:, 0:L], wsl[:, k * 128:(k + 1) * 128],
                         c.ar(16384 + k * 512, 512)[:, 0:L],
                         start=(k == 0), stop=(k == 15))
    c.sig("tensor", "tO")
    c.sig("tensor", "tP")
    if c.on("scalar"):
        c.wt("scalar", "tP")
        c.wt("scalar", "gd", lag=48)
        c.eng.activation(sb["sqt"][:, 0:L], bank[:, 0:L], ACT.Exp,
                         bias=sb["dtb"][:, m:m + 1], scale=1.0)
        c.eng.activation(stg[m % 2][:, 0:L], sb["sqt"][:, 0:L], ACT.Ln,
                         bias=1.0, scale=1.0)
    c.sig("scalar", "eU")
    c.sig("scalar", "sJ")
    if c.on("vector"):
        c.wt("vector", "sJ")
        c.wt("vector", "gd", lag=32)
        c.eng.tensor_tensor(
            sb["sqt2"][:, (m % 2) * 512:(m % 2) * 512 + L],
            stg[m % 2][:, 0:L],
            c.ar(16384 + m * 512, 512)[:, 0:L], ALU.mult)
    c.sig("vector", "vZ")
    c.wt("gpsimd", "vZ")
    c.dma("gpsimd", c.dram["dt"][m * 128:(m + 1) * 128, c0:c0 + L],
          stg[m % 2][:, 0:L], "gd")
    c.dma("gpsimd", c.dram["w"][m * 128:(m + 1) * 128, c0:c0 + L],
          sb["sqt2"][:, (m % 2) * 512:(m % 2) * 512 + L], "gd")


def dt_bc_piece(c: Ctx, c0, L):
    sb, ps = c.sb, c.ps
    bank = ps[2]
    if c.on("tensor"):
        c.wt("tensor", "vAA", lag=1)
        for k in range(16):
            c.eng.matmul(bank[0:32, 0:L], sb["wbc"][:, k * 32:(k + 1) * 32],
                         c.ar(16384 + k * 512, 512)[:, 0:L],
                         start=(k == 0), stop=(k == 15))
    c.sig("tensor", "tQ")
    if c.on("vector"):
        c.wt("vector", "tQ")
        c.eng.tensor_copy(sb["bc"][0:32, c0:c0 + L], bank[0:32, 0:L])
    c.sig("vector", "vAA")
    c.sig("vector", "vY")


def scan_reps(c: Ctx, s0, SL, vaa_snap):
    sb, ps, ins = c.sb, c.ps, c.ins
    reps = c.ar(0, 8192)
    for jj in range(32):
        bank = ps[4 + jj % 2]
        oh = sb["oh"][:, (jj % 2) * 128:(jj % 2) * 128 + 128]
        c.wt("sync", "tR", lag=1)
        c.dma("sync", oh, ins["ohc"][:, jj * 128:(jj + 1) * 128], "ld")
        if c.on("tensor"):
            c.wt_abs("tensor", "vAA", vaa_snap)
            c.wt("tensor", "ld")
            c.wt("tensor", "vAD", lag=1)
            c.eng.matmul(bank[:, 0:SL], oh, sb["bc"][0:32, s0:s0 + SL],
                         start=True, stop=True)
        c.sig("tensor", "tR")
        if c.on("vector"):
            if jj == 0:
                c.wt("vector", "gA")
            c.wt("vector", "tR")
            c.eng.tensor_copy(reps[:, jj * 256:jj * 256 + SL], bank[:, 0:SL])
        c.sig("vector", "vAD")


def scan_dg_piece(c: Ctx, s0, SL, dg, gd_snap):
    sb, ps = c.sb, c.ps
    produce = s0 >= OWN_SC
    reps = c.ar(0, 8192)
    h3f = c.ar(8192, 4096)
    slot = dg % 2
    dtr = c.ar(12288 + slot * 512, 256)[:, 0:SL]
    wr = c.ar(12288 + slot * 512 + 256, 256)[:, 0:SL]
    c.wt_abs("sync", "gd", gd_snap)
    c.wt("sync", "dwf", lag=1)
    c.dma("sync", dtr, c.dram["dt"][dg * 128:(dg + 1) * 128, s0:s0 + SL], "ld")
    c.dma("sync", wr, c.dram["w"][dg * 128:(dg + 1) * 128, s0:s0 + SL], "ld")

    def h3b(n):
        return h3f[:, n * 256:n * 256 + SL]

    GG = 4
    for g in range(NST // GG):
        dslot = c.ar(13312 + (g % 2) * 1024, 1024)

        def dab(j):
            return dslot[:, j * 256:j * 256 + SL]
        if c.on("scalar"):
            c.wt("scalar", "ld")
            c.wt("scalar", "vAE", lag=2)
            for j in range(GG):
                n = g * GG + j
                c.eng.activation(dab(j), dtr, ACT.Exp, bias=0.0,
                                 scale=sb["A"][:, dg * 16 + n:dg * 16 + n + 1])
        c.sig("scalar", "sK")
        if c.on("vector"):
            c.wt("vector", "ld")
            c.wt("vector", "sK")
            for j in range(GG):
                n = g * GG + j
                c.eng.tensor_tensor(h3b(n), wr,
                                    reps[:, n * 256:n * 256 + SL], ALU.mult)
            for j in range(GG):
                n = g * GG + j
                c.eng.tensor_tensor_scan(
                    h3b(n), dab(j), h3b(n),
                    sb["state"][:, dg * 16 + n:dg * 16 + n + 1],
                    ALU.mult, ALU.add)
            for j in range(GG):
                n = g * GG + j
                c.eng.tensor_copy(
                    sb["state"][:, dg * 16 + n:dg * 16 + n + 1],
                    h3f[:, n * 256 + SL - 1:n * 256 + SL])
        c.sig("vector", "vAE")
        if produce:
            if c.on("gpsimd"):
                c.wt("gpsimd", "vAE")
                for j in range(GG):
                    n = g * GG + j
                    c.eng.tensor_tensor(
                        h3b(n), h3b(n),
                        reps[:, (16 + n) * 256:(16 + n) * 256 + SL], ALU.mult)
            c.sig("gpsimd", "gA")
    c.sig("vector", "dwf")
    if not produce:
        return
    if c.on("gpsimd"):
        for j in range(8):
            c.eng.tensor_tensor(h3b(2 * j), h3b(2 * j), h3b(2 * j + 1),
                                ALU.add)
    c.sig("gpsimd", "gB")
    if c.on("vector"):
        c.wt("vector", "gB")
        c.wt("vector", "ld")
        for j in range(4):
            c.eng.tensor_tensor(h3b(4 * j), h3b(4 * j), h3b(4 * j + 2),
                                ALU.add)
        for j in range(2):
            c.eng.tensor_tensor(h3b(8 * j), h3b(8 * j), h3b(8 * j + 4),
                                ALU.add)
        c.eng.tensor_tensor(sb["ystg"][:, 0:SL], h3b(0), h3b(8), ALU.add)
    c.sig("vector", "vAG")
    c.wt("sync", "vAG")
    c.dma("sync", c.dram["y"][dg * 128:(dg + 1) * 128,
                              s0 - OWN_SC:s0 - OWN_SC + SL],
          sb["ystg"][:, 0:SL], "ld")


def ph_dtscan(c: Ctx):
    sb = c.sb
    if c.on("vector"):
        c.eng.memset(sb["state"][:], 0.0)
    snap_gd, snap_vaa = {}, {}

    def dt_piece(i):
        c0, L = SC_CHUNKS[i]
        for m in range(16):
            dt_m_slice(c, c0, L, m)
        snap_gd[i] = c.cnt["gd"]
        dt_bc_piece(c, c0, L)
        snap_vaa[i] = c.cnt["vAA"]

    nch = len(SC_CHUNKS)
    dt_tiles_load(c, *SC_CHUNKS[0])
    dt_piece(0)
    for i in range(nch):
        c0, L = SC_CHUNKS[i]
        subs = [(c0 + off, 256) for off in range(0, L, 256)]
        for si, (s0, SL) in enumerate(subs):
            scan_reps(c, s0, SL, snap_vaa[i])
            if si == 0 and i + 1 < nch:
                dt_tiles_load(c, *SC_CHUNKS[i + 1])
                for dg in range(16):
                    dt_m_slice(c, *SC_CHUNKS[i + 1], dg)
                    scan_dg_piece(c, s0, SL, dg, snap_gd[i])
                snap_gd[i + 1] = c.cnt["gd"]
                dt_bc_piece(c, *SC_CHUNKS[i + 1])
                snap_vaa[i + 1] = c.cnt["vAA"]
            else:
                for dg in range(16):
                    scan_dg_piece(c, s0, SL, dg, snap_gd[i])
    c.barrier()


def ph_out(c: Ctx):
    sb, ps, ins = c.sb, c.ps, c.ins
    woutT = c.ar(0, 16384)
    gts = [c.ar(16384, 2048), c.ar(18432, 2048)]
    xrow = sb["row"][:, 0:1024]
    stg = [sb["row"][:, 1024:1536], sb["row"][:, 1536:2048]]
    for k in range(16):
        c.dma("sync", woutT[:, k * 1024:(k + 1) * 1024],
              ins["w_outT"][k * 128:(k + 1) * 128, :], "ld")
    for tt in range(16):
        slot = tt % 2
        ya = c.ar(20480 + slot * 4096, 2048)
        za = c.ar(20480 + slot * 4096 + 2048, 2048)
        gt = gts[slot]
        c.wt("sync", "vAH", lag=1)
        c.dma("sync", ya.rearrange("p (k q) -> p k q", k=16),
              c.dram["y"][:, tt * 128:(tt + 1) * 128].rearrange(
                  "(k p) q -> p k q", p=128), "ld")
        c.wt("gpsimd", "vAH", lag=1)
        c.dma("gpsimd", za.rearrange("p (k q) -> p k q", k=16),
              c.dram["sz"][:, tt * 128:(tt + 1) * 128].rearrange(
                  "(k p) q -> p k q", p=128), "gd")
        c.wt("sync", "vAJ")
        c.dma("sync", xrow, c.dram["fout"][tt * 128:(tt + 1) * 128, :], "ld")
        if c.on("vector"):
            c.wt("vector", "ld")
            c.wt("vector", "gd")
            c.wt("vector", "tS", lag=1)
            c.eng.tensor_tensor(gt[:], ya, za, ALU.mult)
        c.sig("vector", "vAH")
        c.sig("vector", "vAI")
        for oc in range(2):
            bank = ps[oc]
            if c.on("tensor"):
                c.wt("tensor", "vAI")
                c.wt("tensor", "eV", lag=1)
                for k in range(16):
                    c.eng.matmul(bank[:, 0:512], gt[:, k * 128:(k + 1) * 128],
                                 woutT[:, k * 1024 + oc * 512:
                                       k * 1024 + oc * 512 + 512],
                                 start=(k == 0), stop=(k == 15))
            c.sig("tensor", "tT")
            if c.on("vector"):
                c.wt("vector", "ld")
                c.wt("vector", "tT")
                c.wt("vector", "gd", lag=32)
                c.eng.tensor_tensor(stg[oc][:, 0:512], bank[:, 0:512],
                                    xrow[:, oc * 512:(oc + 1) * 512], ALU.add)
            c.sig("vector", "eV")
            c.sig("vector", "vAJ")
            c.wt("gpsimd", "vAJ")
            c.dma("gpsimd", c.out[tt * 128:(tt + 1) * 128,
                                  oc * 512:(oc + 1) * 512],
                  stg[oc][:, 0:512], "gd")
        c.sig("tensor", "tS")
    if c.on("gpsimd"):
        c.wt("gpsimd", "gd")
    c.barrier()


def _tail_debug(c: Ctx):
    c.dma("gpsimd", c.out[:], c.ins["xs"][0:TOK, :], "gd")
    if c.on("gpsimd"):
        c.wt("gpsimd", "gd")
    c.barrier()


N_PHASES = 10
DBG_DRAM = False
DBG_RACE = False


def emit_all(c: Ctx):
    phases = [ph_consts, ph_ln, ph_ffn1, ph_ffn2, ph_ffn3, ph_ffn4, ph_win,
              ph_dtscan, ph_out]
    for ph in phases[:N_PHASES]:
        ph(c)
    if N_PHASES < 10:
        _tail_debug(c)


SEM_NAMES = [
    "bar", "ld", "gd", "sA", "vA", "vB", "vC", "tA", "vD", "sB", "tB", "vE",
    "vF", "vG", "sC", "vH", "tC", "tD", "sD", "vI", "qA",
    "tE", "vJ0", "vJ", "tF", "vK", "vL", "sE", "tG", "vM", "qB",
    "qC", "vN", "sF", "sF2", "vO",
    "vP", "tH", "vQ", "sG", "vR", "vR0", "tI", "tJ", "vS", "vT", "qD", "vU", "sH",
    "vV", "tK", "qE",
    "vW", "tL", "tM", "eT", "tN", "sI", "qF", "vX",
    "vY", "tO", "tP", "eU", "sJ", "vZ", "qG", "tQ", "vAA",
    "vAB", "vAC", "tR", "vAD", "dwf", "sK", "vAE", "vAF", "gA", "gB", "vAG",
    "vAH", "vAI", "tS", "eV", "tT", "vAJ", "qH", "pv", "ps",
    "au_sync", "au_tensor", "au_vector", "au_scalar", "au_gpsimd",
]


def build_nc():
    nc = bass.Bass("TRN2", target_bir_lowering=False,
                   detect_race_conditions=DBG_RACE)
    ins = {
        "xs": nc.dram_tensor("xs", [TSLAB, DM], F32, kind="ExternalInput").ap(),
        "msk": nc.dram_tensor("msk", [1, SC_LEN], F32, kind="ExternalInput").ap(),
        "w_inT": nc.dram_tensor("w_inT", [DM, 2 * DI], F32, kind="ExternalInput").ap(),
        "w_dtT": nc.dram_tensor("w_dtT", [DI, DI], F32, kind="ExternalInput").ap(),
        "w_bcT": nc.dram_tensor("w_bcT", [128, 512], F32, kind="ExternalInput").ap(),
        "w_outT": nc.dram_tensor("w_outT", [DI, DM], F32, kind="ExternalInput").ap(),
        "w1T": nc.dram_tensor("w1T", [DM, DH], F32, kind="ExternalInput").ap(),
        "w2T": nc.dram_tensor("w2T", [DH, DM], F32, kind="ExternalInput").ap(),
        "alog": nc.dram_tensor("alog", [128, 256], F32, kind="ExternalInput").ap(),
        "cvw": nc.dram_tensor("cvw", [128, 64], F32, kind="ExternalInput").ap(),
        "cvb": nc.dram_tensor("cvb", [128, 16], F32, kind="ExternalInput").ap(),
        "dtb": nc.dram_tensor("dtb", [128, 16], F32, kind="ExternalInput").ap(),
        "fcw": nc.dram_tensor("fcw", [128, 96], F32, kind="ExternalInput").ap(),
        "fcb": nc.dram_tensor("fcb", [128, 32], F32, kind="ExternalInput").ap(),
        "iden": nc.dram_tensor("iden", [128, 128], F32, kind="ExternalInput").ap(),
        "idenb": nc.dram_tensor("idenb", [128, 128], BF16, kind="ExternalInput").ap(),
        "onesr": nc.dram_tensor("onesr", [1, 128], F32, kind="ExternalInput").ap(),
        "onescol": nc.dram_tensor("onescol", [128, 1], F32, kind="ExternalInput").ap(),
        "kconst": nc.dram_tensor("kconst", [128, 4], F32, kind="ExternalInput").ap(),
        "ohc": nc.dram_tensor("ohc", [32, 4096], F32, kind="ExternalInput").ap(),
    }
    out = nc.dram_tensor("o", [TOK, DM], F32, kind="ExternalOutput").ap()
    dk = "ExternalOutput" if DBG_DRAM else "Internal"
    dram = {
        "xnfm": nc.dram_tensor("d_xnfm", [DM, TSLAB], F32, kind=dk).ap(),
        "xconv": nc.dram_tensor("d_xconv", [DI, SC_LEN], F32, kind=dk).ap(),
        "dt": nc.dram_tensor("d_dt", [DI, SC_LEN], F32, kind=dk).ap(),
        "w": nc.dram_tensor("d_w", [DI, SC_LEN], F32, kind=dk).ap(),
        "sz": nc.dram_tensor("d_sz", [DI, TOK], F32, kind=dk).ap(),
        "y": nc.dram_tensor("d_y", [DI, TOK], F32, kind=dk).ap(),
        "h1": nc.dram_tensor("d_h1", [DH, FF_LEN], F32, kind=dk).ap(),
        "hq2": nc.dram_tensor("d_hq2", [DH, TOK], BF16, kind=dk).ap(),
        "fout": nc.dram_tensor("d_fout", [TOK, DM], F32, kind=dk).ap(),
    }
    es = ExitStack()
    with es:
        def sbuf(name, shape, dt=F32):
            return es.enter_context(nc.sbuf_tensor(name, shape, dt))

        sb = {
            "alog": sbuf("s_alog", [128, 256]),
            "A": sbuf("s_A", [128, 256]),
            "cvw": sbuf("s_cvw", [128, 64]),
            "cvb": sbuf("s_cvb", [128, 16]),
            "dtb": sbuf("s_dtb", [128, 16]),
            "fcw": sbuf("s_fcw", [128, 96]),
            "fcb": sbuf("s_fcb", [128, 32]),
            "iden": sbuf("s_iden", [128, 128]),
            "idenb": sbuf("s_idenb", [128, 128], BF16),
            "onesr": sbuf("s_onesr", [1, 128]),
            "onescol": sbuf("s_onescol", [128, 1]),
            "msk": sbuf("s_msk", [1, SC_LEN]),
            "wbc": sbuf("s_wbc", [128, 512]),
            "bc": sbuf("s_bc", [32, SC_LEN]),
            "state": sbuf("s_state", [128, 256]),
            "sw": sbuf("s_sw", [1, 8]),
            "swc": sbuf("s_swc", [128, 8]),
            "acc": sbuf("s_acc", [128, 4]),
            "tmp": sbuf("s_tmp", [128, 8]),
            "sti": sbuf("s_sti", [128, NTT]),
            "q2": sbuf("s_q2", [128, 80]),
            "srow": sbuf("s_srow", [1, TSLAB]),
            "oh": sbuf("s_oh", [32, 256]),
            "kc": sbuf("s_kc", [128, 4]),
            "sqt": sbuf("s_sqt", [128, 2048]),
            "sqt2": sbuf("s_sqt2", [128, 2048]),
            "ki": sbuf("s_ki", [128, 2048], I32),
            "wsl": sbuf("s_wsl", [128, 4096]),
            "wq": sbuf("s_wq", [128, 2048], BF16),
            "row": sbuf("s_row", [128, 2308]),
            "ystg": sbuf("s_ystg", [128, 512]),
            "AR": sbuf("s_AR", [128, ARW]),
        }
        ps = [es.enter_context(nc.psum_tensor(f"ps{i}", [128, 512], F32))
              for i in range(8)]
        nc.const_aps.register(F32, EPS, sb["kc"][:, 0:1])
        sems = {s: es.enter_context(nc.semaphore(s)) for s in SEM_NAMES}

        with nc.Block() as block:
            def mk(en):
                def f(eng):
                    cc = Ctx(en, eng, sems, sb, ps, dram, ins, out)
                    emit_all(cc)
                return f
            block.sync(mk("sync"))
            block.tensor(mk("tensor"))
            block.vector(mk("vector"))
            block.scalar(mk("scalar"))
            block.gpsimd(mk("gpsimd"))
    return nc


def kernel(L0_attn_x, L0_attn_W_in, L0_attn_W_conv, L0_attn_b_conv,
           L0_attn_W_dt, L0_attn_b_dt, L0_attn_W_B, L0_attn_W_C, L0_attn_A_log,
           L0_attn_W_out, L0_ffn_w1, L0_ffn_conv_w, L0_ffn_conv_b, L0_ffn_w2):
    import ml_dtypes
    x = np.ascontiguousarray(np.asarray(L0_attn_x, dtype=np.float32)[0])
    T = x.shape[0]
    cont = np.ascontiguousarray

    def dgmaj(a):
        g = a.shape[0] // 128
        return cont(a.reshape(g, 128, a.shape[1]).transpose(1, 0, 2)
                     .reshape(128, g * a.shape[1]))

    shared = {
        "w_inT": cont(np.asarray(L0_attn_W_in, np.float32).T),
        "w_dtT": cont(np.asarray(L0_attn_W_dt, np.float32).T),
        "w_bcT": dgmaj(np.concatenate([np.asarray(L0_attn_W_B, np.float32),
                                       np.asarray(L0_attn_W_C, np.float32)],
                                      0).T),
        "w_outT": cont(np.asarray(L0_attn_W_out, np.float32).T),
        "w1T": cont(np.asarray(L0_ffn_w1, np.float32).T),
        "w2T": cont(np.asarray(L0_ffn_w2, np.float32).T),
        "alog": dgmaj(np.asarray(L0_attn_A_log, np.float32)),
        "cvw": dgmaj(np.asarray(L0_attn_W_conv, np.float32)[:, 0, :]),
        "cvb": dgmaj(np.asarray(L0_attn_b_conv, np.float32)[:, None]),
        "dtb": dgmaj(np.asarray(L0_attn_b_dt, np.float32)[:, None]),
        "fcw": dgmaj(np.asarray(L0_ffn_conv_w, np.float32)[:, 0, :]),
        "fcb": dgmaj(np.asarray(L0_ffn_conv_b, np.float32)[:, None]),
        "iden": np.eye(128, dtype=np.float32),
        "idenb": np.eye(128, dtype=ml_dtypes.bfloat16),
        "onesr": np.ones((1, 128), np.float32),
        "onescol": np.ones((128, 1), np.float32),
        "kconst": np.full((128, 4), EPS, np.float32),
        "ohc": np.kron(np.eye(32, dtype=np.float32),
                       np.ones((1, 128), np.float32)),
    }

    in_maps = []
    for i in range(NCORES):
        s0 = i * TOK
        lo, hi = s0 - PRE, s0 + TOK + POST
        pl, pr = max(0, -lo), max(0, hi - T)
        slab = np.pad(x[max(lo, 0):min(hi, T)], ((pl, pr), (0, 0)))
        msk = np.zeros((1, SC_LEN), np.float32)
        msk[0, max(0, -(s0 - 256)):] = 1.0
        m = dict(shared)
        m["xs"] = cont(slab)
        m["msk"] = msk
        in_maps.append(m)

    nc = build_nc()
    res = run_bass_kernel_spmd(nc, in_maps, core_ids=list(range(NCORES)))
    outp = np.concatenate([res.results[i]["o"] for i in range(NCORES)], axis=0)
    return outp[None, :, :].astype(np.float32)


# version 13
# speedup vs baseline: 1.0649x; 1.0649x over previous
from contextlib import ExitStack

import numpy as np
import concourse.bass as bass
import concourse.mybir as mybir
from concourse.bass_utils import run_bass_kernel_spmd

F32 = mybir.dt.float32
BF16 = mybir.dt.bfloat16
I32 = mybir.dt.int32
ALU = mybir.AluOpType
ACT = mybir.ActivationFunctionType
AX = mybir.AxisListType

NCORES = 8
DM = 1024
DI = 2048
DH = 4096
NST = 16
TOK = 2048
PRE = 320
POST = 64
TSLAB = PRE + TOK + POST
NTT = TSLAB // 128
XIN_OFF = PRE - 259
XIN_LEN = 3 + 256 + TOK
SC_OFF = PRE - 256
SC_LEN = 256 + TOK
OWN_SC = 256
FF_OFF = PRE - 2
FF_LEN = TOK + 4
EPS = 1e-5

SC_CHUNKS = [(0, 512), (512, 512), (1024, 512), (1536, 512), (2048, 256)]
XIN_CHUNKS = [(0, 512), (512, 512), (1024, 512), (1536, 512), (2048, 259)]
OWN_CHUNKS = [(0, 512), (512, 512), (1024, 512), (1536, 512)]
FF_CHUNKS = [(0, 512), (512, 512), (1024, 512), (1536, 512), (2048, 4)]

ENGS = ("sync", "tensor", "vector", "scalar", "gpsimd")
ARW = 28672


_NO_TRACK = {"wait_ge", "sem_inc", "sem_clear", "wait_op", "drain"}


def _ap_span(ap):
    big = (ap.tensor.name, 0, 1 << 40, 0, 1 << 40)
    try:
        dims = [(int(s), int(c)) for s, c in ap.ap]
        off = int(ap.offset)
    except (TypeError, ValueError, AttributeError):
        return big
    if not dims or any(s < 0 for s, c in dims):
        return big
    ps, pc = dims[0]
    if ps > 0:
        p_lo, c_lo = off // ps, off % ps
    else:
        p_lo, c_lo = 0, off
    c_end = c_lo + sum(s * (c - 1) for s, c in dims[1:]) + 1
    if c_end > ps > 0:
        return big
    sz = mybir.dt.size(ap.dtype)
    return (ap.tensor.name, p_lo, p_lo + pc, c_lo * sz, c_end * sz)


def _overlap(a, b):
    return (a[0] == b[0] and a[1] < b[2] and b[1] < a[2]
            and a[3] < b[4] and b[3] < a[4])


class _EngProxy:

    def __init__(self, eng, ctx):
        object.__setattr__(self, "_e", eng)
        object.__setattr__(self, "_c", ctx)

    def _flush(self):
        c = self._c
        if c.last is None:
            c.dirty = []
            return
        if c.last_sem is None:
            c.cnt[c.auto_sem] += 1
            c.last.then_inc(c.sems[c.auto_sem], 1)
            c.last_sem = c.auto_sem
        self._e.wait_ge(c.sems[c.last_sem], c.cnt[c.last_sem])
        c.dirty = []

    def __getattr__(self, name):
        f = getattr(self._e, name)
        if not callable(f):
            return f
        if name in _NO_TRACK:
            return f

        def g(*a, **k):
            c = self._c
            pos = [x for x in a if isinstance(x, bass.AP)]
            outs, ins_ = [], []
            for key in ("out", "accum_out"):
                v = k.get(key)
                if isinstance(v, bass.AP):
                    outs.append(v)
            if "out" not in k and pos:
                outs.append(pos[0])
                ins_ = pos[1:]
            else:
                ins_ = list(pos)
            for key in ("bias", "scale", "alpha", "initial", "scalar",
                        "scalar1", "scalar2", "in_", "in0", "in1", "mask",
                        "data", "identity"):
                v = k.get(key)
                if isinstance(v, bass.AP):
                    ins_.append(v)
            if name == "memset":
                ins_ = []
            if name.endswith("dma_start"):
                spans = [_ap_span(x) for x in ins_]
                if any(_overlap(s, d) for s in spans for d in c.dirty):
                    self._flush()
                return f(*a, **k)
            if not c.nochk:
                spans = [_ap_span(x) for x in ins_]
                if any(_overlap(s, d) for s in spans for d in c.dirty):
                    self._flush()
            r = f(*a, **k)
            if hasattr(r, "then_inc"):
                c.last = r
                c.last_sem = None
                for o in outs:
                    c.dirty.append(_ap_span(o))
            return r
        return g


class Ctx:

    def __init__(self, eng_name, eng, sems, sb, ps, dram, ins, out):
        self.en = eng_name
        self.raw = eng
        self.last = None
        self.last_sem = None
        self.dirty = []
        self.nochk = False
        self.auto_sem = "au_" + eng_name
        self.eng = _EngProxy(eng, self)
        self.sems = sems
        self.cnt = {k: 0 for k in sems}
        self.nbar = 0
        self.sb = sb
        self.ps = ps
        self.dram = dram
        self.ins = ins
        self.out = out

    def on(self, name):
        return self.en == name

    def ar(self, c0, w, dt=F32):
        a = self.sb["AR"]
        if dt == F32:
            return a[:, c0:c0 + w]
        if dt == BF16:
            return a[:, c0:c0 + (w + 1) // 2].bitcast(BF16)[:, 0:w]
        if dt == I32:
            return a[:, c0:c0 + w].bitcast(I32)
        raise ValueError(dt)

    def dma(self, qeng, out, in_, sem):
        self.cnt[sem] += 16
        if self.en == qeng:
            s = _ap_span(in_)
            if any(_overlap(s, d) for d in self.dirty):
                self.eng._flush()
            i = self.raw.dma_start(out=out, in_=in_)
            i.then_inc(self.sems[sem], 16)

    def flush(self, owner):
        if self.en == owner:
            self.eng._flush()

    def sig(self, owner, sem):
        self.cnt[sem] += 1
        if self.en == owner:
            if self.last is not None and self.last_sem is None:
                self.last.then_inc(self.sems[sem], 1)
                self.last_sem = sem
            elif self.last is not None:
                self.raw.wait_ge(self.sems[self.last_sem],
                                 self.cnt[self.last_sem])
                self.raw.sem_inc(self.sems[sem], 1)
                self.last = None
                self.dirty = []
                self.last_sem = None
            else:
                self.raw.sem_inc(self.sems[sem], 1)

    def ss(self, owner, sem):
        self.sig(owner, sem)
        self.wt(owner, sem)

    def wt(self, waiter, sem, lag=0):
        v = self.cnt[sem] - lag
        if self.en == waiter and v > 0:
            self.raw.wait_ge(self.sems[sem], v)

    def wt_abs(self, waiter, sem, v):
        if self.en == waiter and v > 0:
            self.raw.wait_ge(self.sems[sem], v)

    def barrier(self):
        if self.en == "sync":
            self.wt("sync", "ld")
            self.last = None
        if self.en == "gpsimd":
            self.wt("gpsimd", "gd")
        self.nbar += 1
        if self.last is not None and self.last_sem is None:
            self.last.then_inc(self.sems["bar"], 1)
        elif self.last is not None:
            self.raw.wait_ge(self.sems[self.last_sem], self.cnt[self.last_sem])
            self.raw.sem_inc(self.sems["bar"], 1)
        else:
            self.raw.sem_inc(self.sems["bar"], 1)
        self.last = None
        self.last_sem = None
        self.dirty = []
        self.raw.wait_ge(self.sems["bar"], len(ENGS) * self.nbar)


def ph_consts(c: Ctx):
    sb, ps, ins = c.sb, c.ps, c.ins
    loads = [
        (sb["alog"][:], ins["alog"]),
        (sb["cvw"][:], ins["cvw"]),
        (sb["cvb"][:], ins["cvb"]),
        (sb["dtb"][:], ins["dtb"]),
        (sb["fcw"][:], ins["fcw"]),
        (sb["fcb"][:], ins["fcb"]),
        (sb["iden"][:], ins["iden"]),
        (sb["idenb"][:], ins["idenb"]),
        (sb["onesr"][:], ins["onesr"]),
        (sb["onescol"][:], ins["onescol"]),
        (sb["msk"][:], ins["msk"]),
        (sb["kc"][:], ins["kconst"]),
        (sb["wbc"][:], ins["w_bcT"]),
    ]
    for dst, src in loads:
        c.dma("sync", dst, src, "ld")
    if c.on("scalar"):
        c.wt("scalar", "ld")
        c.eng.activation(sb["A"][:], sb["alog"][:], ACT.Exp, bias=0.0, scale=1.0)
        c.eng.activation(sb["A"][:], sb["A"][:], ACT.Copy, bias=0.0, scale=-1.0)
    c.sig("scalar", "sA")

    if c.on("vector"):
        c.wt("vector", "ld")
        c.eng.memset(sb["acc"][:, 0:2], 0.0)
    c.sig("vector", "vA")
    for wi, (wname, ntile, width) in enumerate([("w1T", 8, DH), ("w2T", 32, DM)]):
        for j in range(ntile):
            slot = j % 2
            wm = c.ar(slot * 8192 // 2, DH)
            c.wt("sync", "vB", lag=1)
            c.dma("sync", wm[:, 0:width], ins[wname][j * 128:(j + 1) * 128, :],
                  "ld")
            if c.on("vector"):
                c.wt("vector", "ld")
                c.eng.tensor_reduce(sb["acc"][:, 2 + wi:3 + wi], wm[:, 0:width],
                                    AX.X, ALU.add, apply_absolute_value=True)
                c.eng.tensor_add(sb["acc"][:, wi:wi + 1], sb["acc"][:, wi:wi + 1],
                                 sb["acc"][:, 2 + wi:3 + wi])
            c.sig("vector", "vB")
    c.sig("vector", "vC")
    if c.on("tensor"):
        c.wt("tensor", "vC")
        c.eng.matmul(ps[7][0:1, 0:1], sb["acc"][:, 0:1], sb["onescol"][:],
                     start=True, stop=True)
        c.eng.matmul(ps[7][0:1, 1:2], sb["acc"][:, 1:2], sb["onescol"][:],
                     start=True, stop=True)
    c.sig("tensor", "tA")
    if c.on("vector"):
        c.wt("vector", "tA")
        c.eng.tensor_scalar(sb["sw"][0:1, 0:2], ps[7][0:1, 0:2],
                            1.0 / (DH * DM), EPS, ALU.mult, ALU.max)
    if c.on("vector"):
        c.eng.reciprocal(sb["sw"][0:1, 2:4], sb["sw"][0:1, 0:2])
    c.sig("vector", "sB")
    if c.on("tensor"):
        c.wt("tensor", "sB")
        c.eng.matmul(ps[7][:, 4:8], sb["onesr"][0:1, :], sb["sw"][0:1, 0:4],
                     start=True, stop=True)
    c.sig("tensor", "tB")
    if c.on("vector"):
        c.wt("vector", "tB")
        c.eng.tensor_copy(sb["swc"][:, 0:4], ps[7][:, 4:8])
    c.sig("vector", "vE")
    c.barrier()


def ph_ln(c: Ctx):
    sb, ps, ins = c.sb, c.ps, c.ins
    xqf = c.ar(18944, 8 * TSLAB, BF16)
    for j in range(NTT):
        slot = j % 2
        xt = sb["row"][:, slot * 1024:(slot + 1) * 1024]
        xn = c.ar(6144 + slot * 1024, DM)
        c.wt("sync", "vF", lag=1)
        c.dma("sync", xt, ins["xs"][j * 128:(j + 1) * 128, :], "ld")
        if c.on("vector"):
            c.wt("vector", "ld")
            c.eng.tensor_reduce(sb["tmp"][:, 0:1], xt, AX.X, ALU.add)
            c.eng.tensor_scalar_mul(sb["tmp"][:, 0:1], sb["tmp"][:, 0:1],
                                    1.0 / DM)
            c.eng.tensor_scalar(xn, xt, sb["tmp"][:, 0:1], None, ALU.subtract,
                                ALU.bypass)
        c.sig("vector", "vF")
        c.sig("vector", "vG")
        if c.on("scalar"):
            c.wt("scalar", "vG")
            c.eng.activation(c.ar(3072, DM), xn, ACT.Square, bias=0.0, scale=1.0,
                             accum_out=sb["tmp"][:, 1:2])
            c.ss("scalar", "ps")
            c.eng.activation(sb["tmp"][:, 2:3], sb["tmp"][:, 1:2], ACT.Sqrt,
                             bias=EPS, scale=1.0 / DM)
        c.sig("scalar", "sC")
        if c.on("vector"):
            c.wt("vector", "sC")
            c.eng.reciprocal(sb["tmp"][:, 6:7], sb["tmp"][:, 2:3])
            c.ss("vector", "pv")
            c.eng.tensor_scalar(xn, xn, sb["tmp"][:, 6:7], None, ALU.mult,
                                ALU.bypass)
            c.eng.tensor_reduce(sb["tmp"][:, 3:4], xn, AX.X, ALU.max,
                                apply_absolute_value=True)
            c.eng.tensor_scalar_max(sb["tmp"][:, 3:4], sb["tmp"][:, 3:4], EPS)
            c.eng.reciprocal(sb["tmp"][:, 4:5], sb["tmp"][:, 3:4])
            c.eng.tensor_scalar_mul(sb["tmp"][:, 5:6], sb["tmp"][:, 4:5], 127.0)
            c.eng.reciprocal(sb["sti"][:, j:j + 1], sb["tmp"][:, 5:6])
            v = c.ar(4096, DM)
            c.ss("vector", "pv")
            c.eng.tensor_scalar(v, xn, sb["tmp"][:, 5:6], None, ALU.mult,
                                ALU.bypass)
            k0 = sb["ki"][:, 0:DM]
            c.eng.tensor_copy(k0, v)
            d = c.ar(5120, DM)
            c.eng.tensor_tensor(d, v, k0, ALU.subtract)
            p_ = c.ar(2048, DM)
            c.eng.tensor_scalar(p_, d, 0.5, None, ALU.is_gt, ALU.bypass)
            c.eng.scalar_tensor_tensor(d, d, -0.5, p_, ALU.is_lt, ALU.subtract)
            xqt = c.ar(0, DM, BF16)
            c.eng.tensor_tensor(xqt, k0, d, ALU.subtract)
        c.sig("vector", "vH")
        if c.on("tensor"):
            c.wt("tensor", "vH")
            c.wt("tensor", "vI")
            c.wt("tensor", "sD")
            for f in range(8):
                bank = ps[f // 4]
                c.eng.transpose(bank[:, (f % 4) * 128:(f % 4) * 128 + 128],
                                xn[:, f * 128:(f + 1) * 128], sb["iden"][:])
            pb = ps[2][:].bitcast(BF16)
            for f in range(8):
                c.eng.transpose(pb[:, f * 128:(f + 1) * 128],
                                c.ar(0, DM, BF16)[:, f * 128:(f + 1) * 128],
                                sb["idenb"][:])
        c.sig("tensor", "tC")
        c.sig("tensor", "tD")
        if c.on("scalar"):
            c.wt("scalar", "tD")
            pb = ps[2][:].bitcast(BF16)
            for f in range(8):
                c.eng.copy(xqf[:, f * TSLAB + j * 128: f * TSLAB + (j + 1) * 128],
                           pb[:, f * 128:(f + 1) * 128])
        c.sig("scalar", "sD")
        if c.on("vector"):
            c.wt("vector", "tC")
            c.wt("vector", "gd")
            for f in range(8):
                bank = ps[f // 4]
                c.eng.tensor_copy(c.ar(1024, 1024)[:, f * 128:(f + 1) * 128],
                                  bank[:, (f % 4) * 128:(f % 4) * 128 + 128])
        c.sig("vector", "vI")
        for h in range(2):
            c.wt("gpsimd", "vI")
            c.dma("gpsimd",
                  c.dram["xnfm"][h * 512:(h + 1) * 512,
                                 j * 128:(j + 1) * 128].rearrange(
                      "(a p) q -> p a q", p=128),
                  c.ar(1024, 1024)[:, h * 512:(h + 1) * 512].rearrange(
                      "p (a q) -> p a q", a=4), "gd")
    c.barrier()


def ph_ffn1(c: Ctx):
    sb, ps, ins = c.sb, c.ps, c.ins
    srep = c.ar(0, 2052)
    stg = [c.ar(2052, 512), c.ar(2564, 512)]
    g01 = c.ar(3076, 1024)
    if c.on("tensor"):
        c.eng.transpose(ps[0][0:NTT, 0:128], sb["sti"][:, 0:NTT], sb["iden"][:])
    c.sig("tensor", "tE")
    if c.on("vector"):
        c.wt("vector", "tE")
        c.eng.tensor_copy(sb["sqt"][0:NTT, 0:128], ps[0][0:NTT, 0:128])
    c.sig("vector", "vJ0")
    c.wt("sync", "vJ0")
    c.dma("sync", sb["srow"][0:1, :], sb["sqt"][0:NTT, 0:128], "ld")
    if c.on("vector"):
        c.wt("vector", "ld")
        c.eng.tensor_scalar(sb["srow"][0:1, :], sb["srow"][0:1, :],
                            sb["sw"][0:1, 0:1], None, ALU.mult, ALU.bypass)
    c.sig("vector", "vJ")
    for ci, (c0, L) in enumerate(FF_CHUNKS):
        bank = ps[4 + ci % 2]
        if c.on("tensor"):
            c.wt("tensor", "vJ")
            c.wt("tensor", "vK", lag=1)
            c.eng.matmul(bank[:, 0:L], sb["onesr"][0:1, :],
                         sb["srow"][0:1, FF_OFF + c0:FF_OFF + c0 + L],
                         start=True, stop=True)
        c.sig("tensor", "tF")
        if c.on("vector"):
            c.wt("vector", "tF")
            c.eng.tensor_copy(srep[:, c0:c0 + L], bank[:, 0:L])
        c.sig("vector", "vK")

    for m in range(32):
        slot = m % 2
        wsl = sb["wsl"][:, slot * 1024:(slot + 1) * 1024]
        c.wt("sync", "sE", lag=1)
        c.dma("sync", wsl.rearrange("p (k m) -> p k m", k=8),
              ins["w1T"][:, m * 128:(m + 1) * 128].rearrange(
                  "(k p) m -> p k m", p=128), "ld")
        if c.on("scalar"):
            c.wt("scalar", "ld")
            c.wt("scalar", "vL")
            c.eng.activation(sb["sqt"][:, 0:1024], wsl, ACT.Abs, bias=0.0,
                             scale=sb["swc"][:, 2:3])
            c.eng.sign(sb["sqt2"][:, 0:1024], wsl)
        c.sig("scalar", "sE")
        if c.on("vector"):
            c.wt("vector", "sE")
            c.eng.tensor_scalar(g01, sb["sqt"][:, 0:1024], 0.5, None, ALU.is_gt,
                                ALU.bypass)
            c.eng.tensor_tensor(sb["wq"][:, slot * 1024:(slot + 1) * 1024],
                                g01, sb["sqt2"][:, 0:1024], ALU.mult)
        c.sig("vector", "vL")
        for ci, (c0, L) in enumerate(FF_CHUNKS):
            bank = ps[(m * len(FF_CHUNKS) + ci) % 2]
            wq = sb["wq"][:, slot * 1024:(slot + 1) * 1024]
            if c.on("tensor"):
                c.wt("tensor", "vL")
                c.wt("tensor", "vM", lag=1)
                for k in range(8):
                    c.eng.matmul(bank[:, 0:L], wq[:, k * 128:(k + 1) * 128],
                                 xqf_slice(c, k, FF_OFF + c0, L),
                                 start=(k == 0), stop=(k == 7))
            c.sig("tensor", "tG")
            if c.on("vector"):
                c.wt("vector", "tG")
                c.wt("vector", "gd", lag=16)
                c.eng.tensor_tensor(stg[ci % 2][:, 0:L], bank[:, 0:L],
                                    srep[:, c0:c0 + L], ALU.mult)
            c.sig("vector", "vM")
            c.wt("gpsimd", "vM")
            c.dma("gpsimd", c.dram["h1"][m * 128:(m + 1) * 128, c0:c0 + L],
                  stg[ci % 2][:, 0:L], "gd")
    c.barrier()


def xqf_slice(c, k, c0, L):
    return c.ar(18944, 8 * TSLAB, BF16)[:, k * TSLAB + c0:k * TSLAB + c0 + L]


def ph_ffn2(c: Ctx):
    sb = c.sb
    hmax = c.ar(4096, 2048)
    sil = c.ar(6144, 2048)
    for g in range(32):
        row = sb["row"][:, 0:FF_LEN]
        c.wt("sync", "vN")
        c.dma("sync", row, c.dram["h1"][g * 128:(g + 1) * 128, :], "ld")
        if c.on("vector"):
            c.wt("vector", "ld")
            c.wt("vector", "sF2")
            acc = sb["sqt"][:, 0:TOK]
            acc2 = sb["sqt2"][:, 0:TOK]
            c.eng.tensor_scalar(acc, row[:, 1:1 + TOK], sb["fcw"][:, g * 3:g * 3 + 1],
                                None, ALU.mult, ALU.bypass)
            c.eng.scalar_tensor_tensor(acc2, row[:, 2:2 + TOK],
                                       sb["fcw"][:, g * 3 + 1:g * 3 + 2], acc,
                                       ALU.mult, ALU.add)
            c.eng.scalar_tensor_tensor(acc, row[:, 3:3 + TOK],
                                       sb["fcw"][:, g * 3 + 2:g * 3 + 3], acc2,
                                       ALU.mult, ALU.add)
        c.sig("vector", "vN")
        if c.on("scalar"):
            c.wt("scalar", "vN")
            c.wt("scalar", "vO")
            c.wt("scalar", "gd")
            c.eng.activation(sil, sb["sqt"][:, 0:TOK], ACT.Silu,
                             bias=sb["fcb"][:, g:g + 1], scale=1.0)
        c.sig("scalar", "sF")
        if c.on("scalar"):
            c.eng.activation(sb["sqt2"][:, 0:TOK], sil, ACT.Abs, bias=0.0,
                             scale=1.0)
        c.sig("scalar", "sF2")
        if c.on("vector"):
            c.wt("vector", "sF2")
            if g == 0:
                c.eng.tensor_copy(hmax, sb["sqt2"][:, 0:TOK])
            else:
                c.eng.tensor_max(hmax, hmax, sb["sqt2"][:, 0:TOK])
        c.sig("vector", "vO")
        c.wt("gpsimd", "vO")
        c.dma("gpsimd", c.dram["h1"][g * 128:(g + 1) * 128, 0:TOK], sil, "gd")
    c.barrier()


def ph_ffn3(c: Ctx):
    sb, ps, ins = c.sb, c.ps, c.ins
    hmax = c.ar(4096, 2048)
    srep2 = c.ar(2048, 2048)
    p_ = c.ar(0, 2048)
    for k in range(16):
        bank = ps[k % 2]
        if c.on("tensor"):
            c.wt("tensor", "vP", lag=1)
            c.eng.transpose(bank[:, 0:128], hmax[:, k * 128:(k + 1) * 128],
                            sb["iden"][:])
        c.sig("tensor", "tH")
        if c.on("vector"):
            c.wt("vector", "tH")
            c.eng.tensor_reduce(sb["q2"][:, k:k + 1], bank[:, 0:128], AX.X,
                                ALU.max)
        c.sig("vector", "vP")
    if c.on("vector"):
        c.eng.tensor_scalar_max(sb["q2"][:, 0:16], sb["q2"][:, 0:16], EPS)
        c.eng.reciprocal(sb["q2"][:, 16:32], sb["q2"][:, 0:16])
        c.eng.tensor_scalar_mul(sb["q2"][:, 32:48], sb["q2"][:, 16:32], 127.0)
        c.eng.reciprocal(sb["q2"][:, 48:64], sb["q2"][:, 32:48])
        c.eng.tensor_scalar(sb["q2"][:, 64:80], sb["q2"][:, 48:64],
                            sb["swc"][:, 1:2], None, ALU.mult, ALU.bypass)
    c.sig("vector", "vR")
    if c.on("tensor"):
        c.wt("tensor", "vR")
        c.eng.transpose(ps[0][0:16, 0:128], sb["q2"][:, 32:48], sb["iden"][:])
    c.sig("tensor", "tI")
    if c.on("vector"):
        c.wt("vector", "tI")
        c.eng.tensor_copy(sb["sqt"][0:16, 0:128], ps[0][0:16, 0:128])
    c.sig("vector", "vR0")
    c.wt("sync", "vR0")
    c.dma("sync", sb["srow"][0:1, 0:TOK], sb["sqt"][0:16, 0:128], "ld")
    for ci, (c0, L) in enumerate(OWN_CHUNKS):
        bank = ps[ci % 2]
        if c.on("tensor"):
            c.wt("tensor", "ld")
            c.wt("tensor", "vS", lag=1)
            c.eng.matmul(bank[:, 0:L], sb["onesr"][0:1, :],
                         sb["srow"][0:1, c0:c0 + L], start=True, stop=True)
        c.sig("tensor", "tJ")
        if c.on("vector"):
            c.wt("vector", "tJ")
            c.eng.tensor_copy(srep2[:, c0:c0 + L], bank[:, 0:L])
        c.sig("vector", "vS")

    for g in range(32):
        row = sb["row"][:, 0:TOK]
        c.wt("sync", "vT", lag=0)
        c.dma("sync", row, c.dram["h1"][g * 128:(g + 1) * 128, 0:TOK], "ld")
        if c.on("vector"):
            c.wt("vector", "ld")
            c.wt("vector", "gd")
            v = sb["sqt"][:, 0:TOK]
            c.eng.tensor_tensor(v, row, srep2, ALU.mult)
            k0 = sb["ki"][:, 0:TOK]
            c.eng.tensor_copy(k0, v)
            d = sb["sqt2"][:, 0:TOK]
            c.eng.tensor_tensor(d, v, k0, ALU.subtract)
            c.eng.tensor_scalar(p_, d, 0.5, None, ALU.is_gt, ALU.bypass)
            c.eng.scalar_tensor_tensor(d, d, -0.5, p_, ALU.is_lt, ALU.subtract)
            c.eng.tensor_tensor(sb["wq"][:, 0:2048], k0, d, ALU.subtract)
        c.sig("vector", "vT")
        c.wt("gpsimd", "vT")
        c.dma("gpsimd", c.dram["hq2"][g * 128:(g + 1) * 128, :],
              sb["wq"][:, 0:2048], "gd")
    for k in range(32):
        slot = k % 2
        wsl = sb["wsl"][:, slot * 1024:(slot + 1) * 1024]
        c.wt("sync", "sH", lag=1)
        c.dma("sync", wsl, ins["w2T"][k * 128:(k + 1) * 128, :], "ld")
        if c.on("scalar"):
            c.wt("scalar", "ld")
            c.wt("scalar", "vU")
            c.wt("scalar", "vT")
            c.eng.activation(sb["sqt"][:, 0:1024], wsl, ACT.Abs, bias=0.0,
                             scale=sb["swc"][:, 3:4])
            c.eng.sign(sb["sqt2"][:, 0:1024], wsl)
        c.sig("scalar", "sH")
        if c.on("vector"):
            c.wt("vector", "sH")
            c.eng.tensor_scalar(p_[:, 0:1024], sb["sqt"][:, 0:1024], 0.5, None,
                                ALU.is_gt, ALU.bypass)
            c.eng.tensor_tensor(c.ar(6144, 32768, BF16)[:, k * 1024:(k + 1) * 1024],
                                p_[:, 0:1024], sb["sqt2"][:, 0:1024], ALU.mult)
        c.sig("vector", "vU")
    c.barrier()


def ph_ffn4(c: Ctx):
    sb, ps, ins = c.sb, c.ps, c.ins
    w2q = c.ar(6144, 32768, BF16)
    lhs = c.ar(0, 4096, BF16)
    stg = [c.ar(2048, 512), c.ar(2560, 512)]
    for tt in range(16):
        xrow = sb["row"][:, 0:1024]
        c.wt("sync", "vV", lag=0)
        c.dma("sync", xrow, ins["xs"][PRE + tt * 128:PRE + (tt + 1) * 128, :],
              "ld")
        for k in range(32):
            c.dma("sync", lhs[:, k * 128:(k + 1) * 128],
                  c.dram["hq2"][k * 128:(k + 1) * 128,
                                tt * 128:(tt + 1) * 128], "ld")
        for oc in range(2):
            bank = ps[oc]
            if c.on("tensor"):
                c.wt("tensor", "ld")
                c.wt("tensor", "vV", lag=1)
                for k in range(32):
                    c.eng.matmul(bank[:, 0:512], lhs[:, k * 128:(k + 1) * 128],
                                 w2q[:, k * 1024 + oc * 512:
                                     k * 1024 + oc * 512 + 512],
                                 start=(k == 0), stop=(k == 31))
            c.sig("tensor", "tK")
            if c.on("vector"):
                c.wt("vector", "tK")
                c.wt("vector", "gd", lag=16)
                c.eng.scalar_tensor_tensor(stg[oc][:, 0:512], bank[:, 0:512],
                                           sb["q2"][:, 64 + tt:65 + tt],
                                           xrow[:, oc * 512:(oc + 1) * 512],
                                           ALU.mult, ALU.add)
            c.sig("vector", "vV")
            c.wt("gpsimd", "vV")
            c.dma("gpsimd", c.dram["fout"][tt * 128:(tt + 1) * 128,
                                           oc * 512:(oc + 1) * 512],
                  stg[oc][:, 0:512], "gd")
    c.barrier()


def ph_win(c: Ctx):
    sb, ps, ins = c.sb, c.ps, c.ins
    bigx = c.ar(0, 19456)
    mrep = c.ar(19456, 2304)
    stg = [c.ar(21760, 512), c.ar(22272, 512)]
    for f in range(8):
        c.dma("sync", bigx[:, f * TSLAB:(f + 1) * TSLAB],
              c.dram["xnfm"][f * 128:(f + 1) * 128, :], "ld")
    for ci, (c0, L) in enumerate(SC_CHUNKS):
        bank = ps[2 + ci % 2]
        if c.on("tensor"):
            c.wt("tensor", "ld")
            c.wt("tensor", "vW", lag=1)
            c.eng.matmul(bank[:, 0:L], sb["onesr"][0:1, :],
                         sb["msk"][0:1, c0:c0 + L], start=True, stop=True)
        c.sig("tensor", "tL")
        if c.on("vector"):
            c.wt("vector", "tL")
            c.eng.tensor_copy(mrep[:, c0:c0 + L], bank[:, 0:L])
        c.sig("vector", "vW")

    for m in range(32):
        slot = m % 2
        wsl = sb["wsl"][:, slot * 1024:(slot + 1) * 1024]
        c.wt("sync", "tM", lag=1)
        c.dma("sync", wsl.rearrange("p (k m) -> p k m", k=8),
              ins["w_inT"][:, m * 128:(m + 1) * 128].rearrange(
                  "(k p) m -> p k m", p=128), "ld")
        chunks = XIN_CHUNKS if m < 16 else OWN_CHUNKS
        base = XIN_OFF if m < 16 else PRE
        for ci, (c0, L) in enumerate(chunks):
            bank = ps[ci % 2]
            if c.on("tensor"):
                c.wt("tensor", "ld")
                c.wt("tensor", "eT", lag=1)
                for k in range(8):
                    c.eng.matmul(bank[:, 0:L], wsl[:, k * 128:(k + 1) * 128],
                                 bigx[:, k * TSLAB + base + c0:
                                      k * TSLAB + base + c0 + L],
                                 start=(k == 0), stop=(k == 7))
            c.sig("tensor", "tN")
            if m < 16:
                if c.on("vector"):
                    c.wt("vector", "tN")
                    c.eng.tensor_copy(sb["row"][:, c0:c0 + L], bank[:, 0:L])
                c.sig("vector", "eT")
            else:
                if c.on("scalar"):
                    c.wt("scalar", "tN")
                    c.wt("scalar", "gd", lag=16)
                    c.eng.activation(stg[ci % 2][:, 0:L], bank[:, 0:L], ACT.Silu,
                                     bias=0.0, scale=1.0)
                c.sig("scalar", "eT")
                c.sig("scalar", "sI")
                c.wt("gpsimd", "sI")
                c.dma("gpsimd",
                      c.dram["sz"][(m - 16) * 128:(m - 15) * 128, c0:c0 + L],
                      stg[ci % 2][:, 0:L], "gd")
        if m < 16:
            for ci, (c0, L) in enumerate(SC_CHUNKS):
                if c.on("vector"):
                    c.wt("vector", "sI")
                    acc = sb["sqt"][:, 0:512]
                    acc2 = sb["sqt2"][:, 0:512]
                    row = sb["row"]
                    c.eng.tensor_scalar(acc[:, 0:L], row[:, c0:c0 + L],
                                        sb["cvw"][:, m * 4:m * 4 + 1], None,
                                        ALU.mult, ALU.bypass)
                    c.eng.scalar_tensor_tensor(
                        acc2[:, 0:L], row[:, c0 + 1:c0 + 1 + L],
                        sb["cvw"][:, m * 4 + 1:m * 4 + 2], acc[:, 0:L],
                        ALU.mult, ALU.add)
                    c.eng.scalar_tensor_tensor(
                        acc[:, 0:L], row[:, c0 + 2:c0 + 2 + L],
                        sb["cvw"][:, m * 4 + 2:m * 4 + 3], acc2[:, 0:L],
                        ALU.mult, ALU.add)
                    c.eng.scalar_tensor_tensor(
                        acc2[:, 0:L], row[:, c0 + 3:c0 + 3 + L],
                        sb["cvw"][:, m * 4 + 3:m * 4 + 4], acc[:, 0:L],
                        ALU.mult, ALU.add)
                    c.eng.tensor_tensor(acc[:, 0:L], acc2[:, 0:L],
                                        mrep[:, c0:c0 + L], ALU.mult)
                c.sig("vector", "vX")
                if c.on("scalar"):
                    c.wt("scalar", "vX")
                    c.wt("scalar", "gd", lag=16)
                    c.eng.activation(stg[ci % 2][:, 0:L], sb["sqt"][:, 0:L],
                                     ACT.Silu, bias=sb["cvb"][:, m:m + 1],
                                     scale=1.0)
                c.sig("scalar", "sI")
                c.wt("gpsimd", "sI")
                c.dma("gpsimd",
                      c.dram["xconv"][m * 128:(m + 1) * 128, c0:c0 + L],
                      stg[ci % 2][:, 0:L], "gd")
            c.sig("tensor", "tM")
        else:
            c.sig("tensor", "tM")
    c.barrier()


DT_CHUNKS = [(o, 256) for o in range(0, SC_LEN, 256)]


def dt_tiles_load(c: Ctx, c0, L):
    for k in range(16):
        c.wt("sync", "vY")
        c.dma("sync", c.ar(24576 + k * 256, 256)[:, 0:L],
              c.dram["xconv"][k * 128:(k + 1) * 128, c0:c0 + L], "ld")


def dt_m_slice(c: Ctx, c0, L, m):
    sb, ps, ins = c.sb, c.ps, c.ins
    wqf = sb["wq"][:].bitcast(F32)
    stg = [wqf[:, 0:512], wqf[:, 512:1024]]
    slot = m % 2
    wsl = sb["wsl"][:, slot * 2048:slot * 2048 + 2048]
    c.wt("sync", "tO", lag=1)
    c.dma("sync", wsl.rearrange("p (k m) -> p k m", k=16),
          ins["w_dtT"][:, m * 128:(m + 1) * 128].rearrange(
              "(k p) m -> p k m", p=128), "ld")
    bank = ps[m % 2]
    if c.on("tensor"):
        c.wt("tensor", "ld")
        c.wt("tensor", "eU", lag=1)
        for k in range(16):
            c.eng.matmul(bank[:, 0:L], wsl[:, k * 128:(k + 1) * 128],
                         c.ar(24576 + k * 256, 256)[:, 0:L],
                         start=(k == 0), stop=(k == 15))
    c.sig("tensor", "tO")
    c.sig("tensor", "tP")
    if c.on("scalar"):
        c.wt("scalar", "tP")
        c.wt("scalar", "gd", lag=48)
        c.eng.activation(sb["sqt"][:, 0:L], bank[:, 0:L], ACT.Exp,
                         bias=sb["dtb"][:, m:m + 1], scale=1.0)
        c.eng.activation(stg[m % 2][:, 0:L], sb["sqt"][:, 0:L], ACT.Ln,
                         bias=1.0, scale=1.0)
    c.sig("scalar", "eU")
    c.sig("scalar", "sJ")
    if c.on("vector"):
        c.wt("vector", "sJ")
        c.wt("vector", "gd", lag=32)
        c.eng.tensor_tensor(
            sb["sqt2"][:, (m % 2) * 512:(m % 2) * 512 + L],
            stg[m % 2][:, 0:L],
            c.ar(24576 + m * 256, 256)[:, 0:L], ALU.mult)
    c.sig("vector", "vZ")
    c.wt("gpsimd", "vZ")
    c.dma("gpsimd", c.dram["dt"][m * 128:(m + 1) * 128, c0:c0 + L],
          stg[m % 2][:, 0:L], "gd")
    c.dma("gpsimd", c.dram["w"][m * 128:(m + 1) * 128, c0:c0 + L],
          sb["sqt2"][:, (m % 2) * 512:(m % 2) * 512 + L], "gd")


def dt_bc_piece(c: Ctx, c0, L):
    sb, ps = c.sb, c.ps
    bank = ps[2]
    if c.on("tensor"):
        c.wt("tensor", "vAA", lag=1)
        for k in range(16):
            c.eng.matmul(bank[0:32, 0:L], sb["wbc"][:, k * 32:(k + 1) * 32],
                         c.ar(24576 + k * 256, 256)[:, 0:L],
                         start=(k == 0), stop=(k == 15))
    c.sig("tensor", "tQ")
    if c.on("vector"):
        c.wt("vector", "tQ")
        c.eng.tensor_copy(sb["bc"][0:32, c0:c0 + L], bank[0:32, 0:L])
    c.sig("vector", "vAA")
    c.sig("vector", "vY")


def scan_reps(c: Ctx, s0, SL, vaa_snap):
    sb, ps, ins = c.sb, c.ps, c.ins
    reps = c.ar(0, 16384)
    for jj in range(32):
        bank = ps[4 + jj % 2]
        oh = sb["oh"][:, (jj % 2) * 128:(jj % 2) * 128 + 128]
        c.wt("sync", "tR", lag=1)
        c.dma("sync", oh, ins["ohc"][:, jj * 128:(jj + 1) * 128], "ld")
        if c.on("tensor"):
            c.wt_abs("tensor", "vAA", vaa_snap)
            c.wt("tensor", "ld")
            c.wt("tensor", "vAD", lag=1)
            c.eng.matmul(bank[:, 0:SL], oh, sb["bc"][0:32, s0:s0 + SL],
                         start=True, stop=True)
        c.sig("tensor", "tR")
        if c.on("vector"):
            if jj == 0:
                c.wt("vector", "gA")
            c.wt("vector", "tR")
            c.eng.tensor_copy(reps[:, jj * 512:jj * 512 + SL], bank[:, 0:SL])
        c.sig("vector", "vAD")


def scan_dg_piece(c: Ctx, s0, SL, dg, gd_snap):
    sb, ps = c.sb, c.ps
    own0 = max(OWN_SC - s0, 0)
    Lo = SL - own0
    reps = c.ar(0, 16384)
    h3f = c.ar(16384, 8192)
    kif = sb["ki"][:].bitcast(F32)
    slot = dg % 2
    dtr = kif[:, slot * 1024:slot * 1024 + SL]
    wr = kif[:, slot * 1024 + 512:slot * 1024 + 512 + SL]
    c.wt_abs("sync", "gd", gd_snap)
    c.wt("sync", "dwf", lag=1)
    c.dma("sync", dtr, c.dram["dt"][dg * 128:(dg + 1) * 128, s0:s0 + SL], "ld")
    c.dma("sync", wr, c.dram["w"][dg * 128:(dg + 1) * 128, s0:s0 + SL], "ld")

    def h3b(n):
        return h3f[:, n * 512:n * 512 + SL]

    GG = 4
    sqf = [sb["row"][:, 0:2048],
           None]
    for g in range(NST // GG):

        def dab(j):
            if g % 2 == 0:
                return sb["row"][:, j * 512:j * 512 + SL]
            t = [sb["sqt"], sb["sqt"], sb["sqt2"], sb["sqt2"]][j]
            o = [1024, 1536, 1024, 1536][j]
            return t[:, o:o + SL]
        if c.on("scalar"):
            c.wt("scalar", "ld")
            c.wt("scalar", "vAE", lag=2)
            for j in range(GG):
                n = g * GG + j
                c.eng.activation(dab(j), dtr, ACT.Exp, bias=0.0,
                                 scale=sb["A"][:, dg * 16 + n:dg * 16 + n + 1])
        c.sig("scalar", "sK")
        if c.on("vector"):
            c.wt("vector", "ld")
            c.wt("vector", "sK")
            for j in range(GG):
                n = g * GG + j
                c.eng.tensor_tensor(h3b(n), wr,
                                    reps[:, n * 512:n * 512 + SL], ALU.mult)
            for j in range(GG):
                n = g * GG + j
                c.eng.tensor_tensor_scan(
                    h3b(n), dab(j), h3b(n),
                    sb["state"][:, dg * 16 + n:dg * 16 + n + 1],
                    ALU.mult, ALU.add)
            for j in range(GG):
                n = g * GG + j
                c.eng.tensor_copy(
                    sb["state"][:, dg * 16 + n:dg * 16 + n + 1],
                    h3f[:, n * 512 + SL - 1:n * 512 + SL])
        c.sig("vector", "vAE")
        if c.on("gpsimd"):
            c.wt("gpsimd", "vAE")
            for j in range(GG):
                n = g * GG + j
                c.eng.tensor_tensor(
                    h3b(n)[:, own0:SL], h3b(n)[:, own0:SL],
                    reps[:, (16 + n) * 512 + own0:(16 + n) * 512 + SL],
                    ALU.mult)
        c.sig("gpsimd", "gA")
    c.sig("vector", "dwf")
    if c.on("gpsimd"):
        for j in range(8):
            c.eng.tensor_tensor(h3b(2 * j)[:, own0:SL], h3b(2 * j)[:, own0:SL],
                                h3b(2 * j + 1)[:, own0:SL], ALU.add)
    c.sig("gpsimd", "gB")
    if c.on("vector"):
        c.wt("vector", "gB")
        c.wt("vector", "ld")
        for j in range(4):
            c.eng.tensor_tensor(h3b(4 * j)[:, own0:SL], h3b(4 * j)[:, own0:SL],
                                h3b(4 * j + 2)[:, own0:SL], ALU.add)
        for j in range(2):
            c.eng.tensor_tensor(h3b(8 * j)[:, own0:SL], h3b(8 * j)[:, own0:SL],
                                h3b(8 * j + 4)[:, own0:SL], ALU.add)
        c.eng.tensor_tensor(sb["ystg"][:, 0:Lo], h3b(0)[:, own0:SL],
                            h3b(8)[:, own0:SL], ALU.add)
    c.sig("vector", "vAG")
    c.wt("sync", "vAG")
    c.dma("sync", c.dram["y"][dg * 128:(dg + 1) * 128,
                              s0 + own0 - OWN_SC:s0 + SL - OWN_SC],
          sb["ystg"][:, 0:Lo], "ld")


def ph_dtscan(c: Ctx):
    sb = c.sb
    if c.on("vector"):
        c.eng.memset(sb["state"][:], 0.0)
    snap_gd, snap_vaa = {}, {}

    def dt_halves(i):
        c0, L = SC_CHUNKS[i]
        for h0 in range(c0, c0 + L, 256):
            hL = min(256, c0 + L - h0)
            yield lambda h0=h0, hL=hL: dt_tiles_load(c, h0, hL)
            for m in range(16):
                yield (lambda h0=h0, hL=hL, m=m: dt_m_slice(c, h0, hL, m))
            yield lambda h0=h0, hL=hL: dt_bc_piece(c, h0, hL)

    def run_all(gen):
        for t in gen:
            t()

    nch = len(SC_CHUNKS)
    run_all(dt_halves(0))
    snap_gd[0] = c.cnt["gd"]
    snap_vaa[0] = c.cnt["vAA"]
    for i in range(nch):
        c0, L = SC_CHUNKS[i]
        scan_reps(c, c0, L, snap_vaa[i])
        if i + 1 < nch:
            thunks = list(dt_halves(i + 1))
            per = (len(thunks) + 15) // 16
            for dg in range(16):
                for t in thunks[dg * per:(dg + 1) * per]:
                    t()
                scan_dg_piece(c, c0, L, dg, snap_gd[i])
            snap_gd[i + 1] = c.cnt["gd"]
            snap_vaa[i + 1] = c.cnt["vAA"]
        else:
            for dg in range(16):
                scan_dg_piece(c, c0, L, dg, snap_gd[i])
    c.barrier()


def ph_out(c: Ctx):
    sb, ps, ins = c.sb, c.ps, c.ins
    woutT = c.ar(0, 16384)
    gts = [c.ar(16384, 2048), c.ar(18432, 2048)]
    xrow = sb["row"][:, 0:1024]
    stg = [sb["row"][:, 1024:1536], sb["row"][:, 1536:2048]]
    for k in range(16):
        c.dma("sync", woutT[:, k * 1024:(k + 1) * 1024],
              ins["w_outT"][k * 128:(k + 1) * 128, :], "ld")
    for tt in range(16):
        slot = tt % 2
        ya = c.ar(20480 + slot * 4096, 2048)
        za = c.ar(20480 + slot * 4096 + 2048, 2048)
        gt = gts[slot]
        c.wt("sync", "vAH", lag=1)
        c.dma("sync", ya.rearrange("p (k q) -> p k q", k=16),
              c.dram["y"][:, tt * 128:(tt + 1) * 128].rearrange(
                  "(k p) q -> p k q", p=128), "ld")
        c.wt("gpsimd", "vAH", lag=1)
        c.dma("gpsimd", za.rearrange("p (k q) -> p k q", k=16),
              c.dram["sz"][:, tt * 128:(tt + 1) * 128].rearrange(
                  "(k p) q -> p k q", p=128), "gd")
        c.wt("sync", "vAJ")
        c.dma("sync", xrow, c.dram["fout"][tt * 128:(tt + 1) * 128, :], "ld")
        if c.on("vector"):
            c.wt("vector", "ld")
            c.wt("vector", "gd")
            c.wt("vector", "tS", lag=1)
            c.eng.tensor_tensor(gt[:], ya, za, ALU.mult)
        c.sig("vector", "vAH")
        c.sig("vector", "vAI")
        for oc in range(2):
            bank = ps[oc]
            if c.on("tensor"):
                c.wt("tensor", "vAI")
                c.wt("tensor", "eV", lag=1)
                for k in range(16):
                    c.eng.matmul(bank[:, 0:512], gt[:, k * 128:(k + 1) * 128],
                                 woutT[:, k * 1024 + oc * 512:
                                       k * 1024 + oc * 512 + 512],
                                 start=(k == 0), stop=(k == 15))
            c.sig("tensor", "tT")
            if c.on("vector"):
                c.wt("vector", "ld")
                c.wt("vector", "tT")
                c.wt("vector", "gd", lag=32)
                c.eng.tensor_tensor(stg[oc][:, 0:512], bank[:, 0:512],
                                    xrow[:, oc * 512:(oc + 1) * 512], ALU.add)
            c.sig("vector", "eV")
            c.sig("vector", "vAJ")
            c.wt("gpsimd", "vAJ")
            c.dma("gpsimd", c.out[tt * 128:(tt + 1) * 128,
                                  oc * 512:(oc + 1) * 512],
                  stg[oc][:, 0:512], "gd")
        c.sig("tensor", "tS")
    if c.on("gpsimd"):
        c.wt("gpsimd", "gd")
    c.barrier()


def _tail_debug(c: Ctx):
    c.dma("gpsimd", c.out[:], c.ins["xs"][0:TOK, :], "gd")
    if c.on("gpsimd"):
        c.wt("gpsimd", "gd")
    c.barrier()


N_PHASES = 10
DBG_DRAM = False
DBG_RACE = False


def emit_all(c: Ctx):
    phases = [ph_consts, ph_ln, ph_ffn1, ph_ffn2, ph_ffn3, ph_ffn4, ph_win,
              ph_dtscan, ph_out]
    for ph in phases[:N_PHASES]:
        ph(c)
    if N_PHASES < 10:
        _tail_debug(c)


SEM_NAMES = [
    "bar", "ld", "gd", "sA", "vA", "vB", "vC", "tA", "vD", "sB", "tB", "vE",
    "vF", "vG", "sC", "vH", "tC", "tD", "sD", "vI", "qA",
    "tE", "vJ0", "vJ", "tF", "vK", "vL", "sE", "tG", "vM", "qB",
    "qC", "vN", "sF", "sF2", "vO",
    "vP", "tH", "vQ", "sG", "vR", "vR0", "tI", "tJ", "vS", "vT", "qD", "vU", "sH",
    "vV", "tK", "qE",
    "vW", "tL", "tM", "eT", "tN", "sI", "qF", "vX",
    "vY", "tO", "tP", "eU", "sJ", "vZ", "qG", "tQ", "vAA",
    "vAB", "vAC", "tR", "vAD", "dwf", "sK", "vAE", "vAF", "gA", "gB", "vAG",
    "vAH", "vAI", "tS", "eV", "tT", "vAJ", "qH", "pv", "ps",
    "au_sync", "au_tensor", "au_vector", "au_scalar", "au_gpsimd",
]


def build_nc():
    nc = bass.Bass("TRN2", target_bir_lowering=False,
                   detect_race_conditions=DBG_RACE)
    ins = {
        "xs": nc.dram_tensor("xs", [TSLAB, DM], F32, kind="ExternalInput").ap(),
        "msk": nc.dram_tensor("msk", [1, SC_LEN], F32, kind="ExternalInput").ap(),
        "w_inT": nc.dram_tensor("w_inT", [DM, 2 * DI], F32, kind="ExternalInput").ap(),
        "w_dtT": nc.dram_tensor("w_dtT", [DI, DI], F32, kind="ExternalInput").ap(),
        "w_bcT": nc.dram_tensor("w_bcT", [128, 512], F32, kind="ExternalInput").ap(),
        "w_outT": nc.dram_tensor("w_outT", [DI, DM], F32, kind="ExternalInput").ap(),
        "w1T": nc.dram_tensor("w1T", [DM, DH], F32, kind="ExternalInput").ap(),
        "w2T": nc.dram_tensor("w2T", [DH, DM], F32, kind="ExternalInput").ap(),
        "alog": nc.dram_tensor("alog", [128, 256], F32, kind="ExternalInput").ap(),
        "cvw": nc.dram_tensor("cvw", [128, 64], F32, kind="ExternalInput").ap(),
        "cvb": nc.dram_tensor("cvb", [128, 16], F32, kind="ExternalInput").ap(),
        "dtb": nc.dram_tensor("dtb", [128, 16], F32, kind="ExternalInput").ap(),
        "fcw": nc.dram_tensor("fcw", [128, 96], F32, kind="ExternalInput").ap(),
        "fcb": nc.dram_tensor("fcb", [128, 32], F32, kind="ExternalInput").ap(),
        "iden": nc.dram_tensor("iden", [128, 128], F32, kind="ExternalInput").ap(),
        "idenb": nc.dram_tensor("idenb", [128, 128], BF16, kind="ExternalInput").ap(),
        "onesr": nc.dram_tensor("onesr", [1, 128], F32, kind="ExternalInput").ap(),
        "onescol": nc.dram_tensor("onescol", [128, 1], F32, kind="ExternalInput").ap(),
        "kconst": nc.dram_tensor("kconst", [128, 4], F32, kind="ExternalInput").ap(),
        "ohc": nc.dram_tensor("ohc", [32, 4096], F32, kind="ExternalInput").ap(),
    }
    out = nc.dram_tensor("o", [TOK, DM], F32, kind="ExternalOutput").ap()
    dk = "ExternalOutput" if DBG_DRAM else "Internal"
    dram = {
        "xnfm": nc.dram_tensor("d_xnfm", [DM, TSLAB], F32, kind=dk).ap(),
        "xconv": nc.dram_tensor("d_xconv", [DI, SC_LEN], F32, kind=dk).ap(),
        "dt": nc.dram_tensor("d_dt", [DI, SC_LEN], F32, kind=dk).ap(),
        "w": nc.dram_tensor("d_w", [DI, SC_LEN], F32, kind=dk).ap(),
        "sz": nc.dram_tensor("d_sz", [DI, TOK], F32, kind=dk).ap(),
        "y": nc.dram_tensor("d_y", [DI, TOK], F32, kind=dk).ap(),
        "h1": nc.dram_tensor("d_h1", [DH, FF_LEN], F32, kind=dk).ap(),
        "hq2": nc.dram_tensor("d_hq2", [DH, TOK], BF16, kind=dk).ap(),
        "fout": nc.dram_tensor("d_fout", [TOK, DM], F32, kind=dk).ap(),
    }
    es = ExitStack()
    with es:
        def sbuf(name, shape, dt=F32):
            return es.enter_context(nc.sbuf_tensor(name, shape, dt))

        sb = {
            "alog": sbuf("s_alog", [128, 256]),
            "A": sbuf("s_A", [128, 256]),
            "cvw": sbuf("s_cvw", [128, 64]),
            "cvb": sbuf("s_cvb", [128, 16]),
            "dtb": sbuf("s_dtb", [128, 16]),
            "fcw": sbuf("s_fcw", [128, 96]),
            "fcb": sbuf("s_fcb", [128, 32]),
            "iden": sbuf("s_iden", [128, 128]),
            "idenb": sbuf("s_idenb", [128, 128], BF16),
            "onesr": sbuf("s_onesr", [1, 128]),
            "onescol": sbuf("s_onescol", [128, 1]),
            "msk": sbuf("s_msk", [1, SC_LEN]),
            "wbc": sbuf("s_wbc", [128, 512]),
            "bc": sbuf("s_bc", [32, SC_LEN]),
            "state": sbuf("s_state", [128, 256]),
            "sw": sbuf("s_sw", [1, 8]),
            "swc": sbuf("s_swc", [128, 8]),
            "acc": sbuf("s_acc", [128, 4]),
            "tmp": sbuf("s_tmp", [128, 8]),
            "sti": sbuf("s_sti", [128, NTT]),
            "q2": sbuf("s_q2", [128, 80]),
            "srow": sbuf("s_srow", [1, TSLAB]),
            "oh": sbuf("s_oh", [32, 256]),
            "kc": sbuf("s_kc", [128, 4]),
            "sqt": sbuf("s_sqt", [128, 2048]),
            "sqt2": sbuf("s_sqt2", [128, 2048]),
            "ki": sbuf("s_ki", [128, 2048], I32),
            "wsl": sbuf("s_wsl", [128, 4096]),
            "wq": sbuf("s_wq", [128, 2048], BF16),
            "row": sbuf("s_row", [128, 2308]),
            "ystg": sbuf("s_ystg", [128, 512]),
            "AR": sbuf("s_AR", [128, ARW]),
        }
        ps = [es.enter_context(nc.psum_tensor(f"ps{i}", [128, 512], F32))
              for i in range(8)]
        nc.const_aps.register(F32, EPS, sb["kc"][:, 0:1])
        sems = {s: es.enter_context(nc.semaphore(s)) for s in SEM_NAMES}

        with nc.Block() as block:
            def mk(en):
                def f(eng):
                    cc = Ctx(en, eng, sems, sb, ps, dram, ins, out)
                    emit_all(cc)
                return f
            block.sync(mk("sync"))
            block.tensor(mk("tensor"))
            block.vector(mk("vector"))
            block.scalar(mk("scalar"))
            block.gpsimd(mk("gpsimd"))
    return nc


def kernel(L0_attn_x, L0_attn_W_in, L0_attn_W_conv, L0_attn_b_conv,
           L0_attn_W_dt, L0_attn_b_dt, L0_attn_W_B, L0_attn_W_C, L0_attn_A_log,
           L0_attn_W_out, L0_ffn_w1, L0_ffn_conv_w, L0_ffn_conv_b, L0_ffn_w2):
    import ml_dtypes
    x = np.ascontiguousarray(np.asarray(L0_attn_x, dtype=np.float32)[0])
    T = x.shape[0]
    cont = np.ascontiguousarray

    def dgmaj(a):
        g = a.shape[0] // 128
        return cont(a.reshape(g, 128, a.shape[1]).transpose(1, 0, 2)
                     .reshape(128, g * a.shape[1]))

    shared = {
        "w_inT": cont(np.asarray(L0_attn_W_in, np.float32).T),
        "w_dtT": cont(np.asarray(L0_attn_W_dt, np.float32).T),
        "w_bcT": dgmaj(np.concatenate([np.asarray(L0_attn_W_B, np.float32),
                                       np.asarray(L0_attn_W_C, np.float32)],
                                      0).T),
        "w_outT": cont(np.asarray(L0_attn_W_out, np.float32).T),
        "w1T": cont(np.asarray(L0_ffn_w1, np.float32).T),
        "w2T": cont(np.asarray(L0_ffn_w2, np.float32).T),
        "alog": dgmaj(np.asarray(L0_attn_A_log, np.float32)),
        "cvw": dgmaj(np.asarray(L0_attn_W_conv, np.float32)[:, 0, :]),
        "cvb": dgmaj(np.asarray(L0_attn_b_conv, np.float32)[:, None]),
        "dtb": dgmaj(np.asarray(L0_attn_b_dt, np.float32)[:, None]),
        "fcw": dgmaj(np.asarray(L0_ffn_conv_w, np.float32)[:, 0, :]),
        "fcb": dgmaj(np.asarray(L0_ffn_conv_b, np.float32)[:, None]),
        "iden": np.eye(128, dtype=np.float32),
        "idenb": np.eye(128, dtype=ml_dtypes.bfloat16),
        "onesr": np.ones((1, 128), np.float32),
        "onescol": np.ones((128, 1), np.float32),
        "kconst": np.full((128, 4), EPS, np.float32),
        "ohc": np.kron(np.eye(32, dtype=np.float32),
                       np.ones((1, 128), np.float32)),
    }

    in_maps = []
    for i in range(NCORES):
        s0 = i * TOK
        lo, hi = s0 - PRE, s0 + TOK + POST
        pl, pr = max(0, -lo), max(0, hi - T)
        slab = np.pad(x[max(lo, 0):min(hi, T)], ((pl, pr), (0, 0)))
        msk = np.zeros((1, SC_LEN), np.float32)
        msk[0, max(0, -(s0 - 256)):] = 1.0
        m = dict(shared)
        m["xs"] = cont(slab)
        m["msk"] = msk
        in_maps.append(m)

    nc = build_nc()
    res = run_bass_kernel_spmd(nc, in_maps, core_ids=list(range(NCORES)))
    outp = np.concatenate([res.results[i]["o"] for i in range(NCORES)], axis=0)
    return outp[None, :, :].astype(np.float32)


# version 14 (final)
# speedup vs baseline: 1.0700x; 1.0048x over previous
from contextlib import ExitStack

import numpy as np
import concourse.bass as bass
import concourse.mybir as mybir
from concourse.bass_utils import run_bass_kernel_spmd

F32 = mybir.dt.float32
BF16 = mybir.dt.bfloat16
I32 = mybir.dt.int32
ALU = mybir.AluOpType
ACT = mybir.ActivationFunctionType
AX = mybir.AxisListType

NCORES = 8
DM = 1024
DI = 2048
DH = 4096
NST = 16
TOK = 2048
PRE = 320
POST = 64
TSLAB = PRE + TOK + POST
NTT = TSLAB // 128
XIN_OFF = PRE - 259
XIN_LEN = 3 + 256 + TOK
SC_OFF = PRE - 256
SC_LEN = 256 + TOK
OWN_SC = 256
FF_OFF = PRE - 2
FF_LEN = TOK + 4
EPS = 1e-5

SC_CHUNKS = [(0, 512), (512, 512), (1024, 512), (1536, 512), (2048, 256)]
XIN_CHUNKS = [(0, 512), (512, 512), (1024, 512), (1536, 512), (2048, 259)]
OWN_CHUNKS = [(0, 512), (512, 512), (1024, 512), (1536, 512)]
FF_CHUNKS = [(0, 512), (512, 512), (1024, 512), (1536, 512), (2048, 4)]

ENGS = ("sync", "tensor", "vector", "scalar", "gpsimd")
ARW = 28672


_NO_TRACK = {"wait_ge", "sem_inc", "sem_clear", "wait_op", "drain"}


def _ap_span(ap):
    big = (ap.tensor.name, 0, 1 << 40, 0, 1 << 40)
    try:
        dims = [(int(s), int(c)) for s, c in ap.ap]
        off = int(ap.offset)
    except (TypeError, ValueError, AttributeError):
        return big
    if not dims or any(s < 0 for s, c in dims):
        return big
    ps, pc = dims[0]
    if ps > 0:
        p_lo, c_lo = off // ps, off % ps
    else:
        p_lo, c_lo = 0, off
    c_end = c_lo + sum(s * (c - 1) for s, c in dims[1:]) + 1
    if c_end > ps > 0:
        return big
    sz = mybir.dt.size(ap.dtype)
    return (ap.tensor.name, p_lo, p_lo + pc, c_lo * sz, c_end * sz)


def _overlap(a, b):
    return (a[0] == b[0] and a[1] < b[2] and b[1] < a[2]
            and a[3] < b[4] and b[3] < a[4])


class _EngProxy:

    def __init__(self, eng, ctx):
        object.__setattr__(self, "_e", eng)
        object.__setattr__(self, "_c", ctx)

    def _flush(self):
        c = self._c
        if c.last is None:
            c.dirty = []
            return
        if c.last_sem is None:
            c.cnt[c.auto_sem] += 1
            c.last.then_inc(c.sems[c.auto_sem], 1)
            c.last_sem = c.auto_sem
        self._e.wait_ge(c.sems[c.last_sem], c.cnt[c.last_sem])
        c.dirty = []

    def __getattr__(self, name):
        f = getattr(self._e, name)
        if not callable(f):
            return f
        if name in _NO_TRACK:
            return f

        def g(*a, **k):
            c = self._c
            pos = [x for x in a if isinstance(x, bass.AP)]
            outs, ins_ = [], []
            for key in ("out", "accum_out"):
                v = k.get(key)
                if isinstance(v, bass.AP):
                    outs.append(v)
            if "out" not in k and pos:
                outs.append(pos[0])
                ins_ = pos[1:]
            else:
                ins_ = list(pos)
            for key in ("bias", "scale", "alpha", "initial", "scalar",
                        "scalar1", "scalar2", "in_", "in0", "in1", "mask",
                        "data", "identity"):
                v = k.get(key)
                if isinstance(v, bass.AP):
                    ins_.append(v)
            if name == "memset":
                ins_ = []
            if name.endswith("dma_start"):
                spans = [_ap_span(x) for x in ins_]
                if any(_overlap(s, d) for s in spans for d in c.dirty):
                    self._flush()
                return f(*a, **k)
            if not c.nochk:
                spans = [_ap_span(x) for x in ins_]
                if any(_overlap(s, d) for s in spans for d in c.dirty):
                    self._flush()
            r = f(*a, **k)
            if hasattr(r, "then_inc"):
                c.last = r
                c.last_sem = None
                for o in outs:
                    c.dirty.append(_ap_span(o))
            return r
        return g


class Ctx:

    def __init__(self, eng_name, eng, sems, sb, ps, dram, ins, out):
        self.en = eng_name
        self.raw = eng
        self.last = None
        self.last_sem = None
        self.dirty = []
        self.nochk = False
        self.auto_sem = "au_" + eng_name
        self.eng = _EngProxy(eng, self)
        self.sems = sems
        self.cnt = {k: 0 for k in sems}
        self.nbar = 0
        self.sb = sb
        self.ps = ps
        self.dram = dram
        self.ins = ins
        self.out = out

    def on(self, name):
        return self.en == name

    def ar(self, c0, w, dt=F32):
        a = self.sb["AR"]
        if dt == F32:
            return a[:, c0:c0 + w]
        if dt == BF16:
            return a[:, c0:c0 + (w + 1) // 2].bitcast(BF16)[:, 0:w]
        if dt == I32:
            return a[:, c0:c0 + w].bitcast(I32)
        raise ValueError(dt)

    def dma(self, qeng, out, in_, sem):
        self.cnt[sem] += 16
        if self.en == qeng:
            s = _ap_span(in_)
            if any(_overlap(s, d) for d in self.dirty):
                self.eng._flush()
            i = self.raw.dma_start(out=out, in_=in_)
            i.then_inc(self.sems[sem], 16)

    def flush(self, owner):
        if self.en == owner:
            self.eng._flush()

    def sig(self, owner, sem):
        self.cnt[sem] += 1
        if self.en == owner:
            if self.last is not None and self.last_sem is None:
                self.last.then_inc(self.sems[sem], 1)
                self.last_sem = sem
            elif self.last is not None:
                self.raw.wait_ge(self.sems[self.last_sem],
                                 self.cnt[self.last_sem])
                self.raw.sem_inc(self.sems[sem], 1)
                self.last = None
                self.dirty = []
                self.last_sem = None
            else:
                self.raw.sem_inc(self.sems[sem], 1)

    def ss(self, owner, sem):
        self.sig(owner, sem)
        self.wt(owner, sem)

    def wt(self, waiter, sem, lag=0):
        v = self.cnt[sem] - lag
        if self.en == waiter and v > 0:
            self.raw.wait_ge(self.sems[sem], v)

    def wt_abs(self, waiter, sem, v):
        if self.en == waiter and v > 0:
            self.raw.wait_ge(self.sems[sem], v)

    def barrier(self):
        if self.en == "sync":
            self.wt("sync", "ld")
            self.last = None
        if self.en == "gpsimd":
            self.wt("gpsimd", "gd")
        self.nbar += 1
        if self.last is not None and self.last_sem is None:
            self.last.then_inc(self.sems["bar"], 1)
        elif self.last is not None:
            self.raw.wait_ge(self.sems[self.last_sem], self.cnt[self.last_sem])
            self.raw.sem_inc(self.sems["bar"], 1)
        else:
            self.raw.sem_inc(self.sems["bar"], 1)
        self.last = None
        self.last_sem = None
        self.dirty = []
        self.raw.wait_ge(self.sems["bar"], len(ENGS) * self.nbar)


def ph_consts(c: Ctx):
    sb, ps, ins = c.sb, c.ps, c.ins
    loads = [
        (sb["alog"][:], ins["alog"]),
        (sb["cvw"][:], ins["cvw"]),
        (sb["cvb"][:], ins["cvb"]),
        (sb["dtb"][:], ins["dtb"]),
        (sb["fcw"][:], ins["fcw"]),
        (sb["fcb"][:], ins["fcb"]),
        (sb["iden"][:], ins["iden"]),
        (sb["idenb"][:], ins["idenb"]),
        (sb["onesr"][:], ins["onesr"]),
        (sb["onescol"][:], ins["onescol"]),
        (sb["msk"][:], ins["msk"]),
        (sb["kc"][:], ins["kconst"]),
        (sb["wbc"][:], ins["w_bcT"]),
    ]
    for dst, src in loads:
        c.dma("sync", dst, src, "ld")
    if c.on("scalar"):
        c.wt("scalar", "ld")
        c.eng.activation(sb["A"][:], sb["alog"][:], ACT.Exp, bias=0.0, scale=1.0)
        c.eng.activation(sb["A"][:], sb["A"][:], ACT.Copy, bias=0.0, scale=-1.0)
    c.sig("scalar", "sA")

    if c.on("vector"):
        c.wt("vector", "ld")
        c.eng.memset(sb["acc"][:, 0:2], 0.0)
    c.sig("vector", "vA")
    for wi, (wname, ntile, width) in enumerate([("w1T", 8, DH), ("w2T", 32, DM)]):
        for j in range(ntile):
            slot = j % 2
            wm = c.ar(slot * 8192 // 2, DH)
            c.wt("sync", "vB", lag=1)
            c.dma("sync", wm[:, 0:width], ins[wname][j * 128:(j + 1) * 128, :],
                  "ld")
            if c.on("vector"):
                c.wt("vector", "ld")
                c.eng.tensor_reduce(sb["acc"][:, 2 + wi:3 + wi], wm[:, 0:width],
                                    AX.X, ALU.add, apply_absolute_value=True)
                c.eng.tensor_add(sb["acc"][:, wi:wi + 1], sb["acc"][:, wi:wi + 1],
                                 sb["acc"][:, 2 + wi:3 + wi])
            c.sig("vector", "vB")
    c.sig("vector", "vC")
    if c.on("tensor"):
        c.wt("tensor", "vC")
        c.eng.matmul(ps[7][0:1, 0:1], sb["acc"][:, 0:1], sb["onescol"][:],
                     start=True, stop=True)
        c.eng.matmul(ps[7][0:1, 1:2], sb["acc"][:, 1:2], sb["onescol"][:],
                     start=True, stop=True)
    c.sig("tensor", "tA")
    if c.on("vector"):
        c.wt("vector", "tA")
        c.eng.tensor_scalar(sb["sw"][0:1, 0:2], ps[7][0:1, 0:2],
                            1.0 / (DH * DM), EPS, ALU.mult, ALU.max)
    if c.on("vector"):
        c.eng.reciprocal(sb["sw"][0:1, 2:4], sb["sw"][0:1, 0:2])
    c.sig("vector", "sB")
    if c.on("tensor"):
        c.wt("tensor", "sB")
        c.eng.matmul(ps[7][:, 4:8], sb["onesr"][0:1, :], sb["sw"][0:1, 0:4],
                     start=True, stop=True)
    c.sig("tensor", "tB")
    if c.on("vector"):
        c.wt("vector", "tB")
        c.eng.tensor_copy(sb["swc"][:, 0:4], ps[7][:, 4:8])
    c.sig("vector", "vE")
    c.barrier()


def ph_ln(c: Ctx):
    sb, ps, ins = c.sb, c.ps, c.ins
    xqf = c.ar(18944, 8 * TSLAB, BF16)
    for j in range(NTT):
        slot = j % 2
        xt = sb["row"][:, slot * 1024:(slot + 1) * 1024]
        xn = c.ar(6144 + slot * 1024, DM)
        c.wt("sync", "vF", lag=1)
        c.dma("sync", xt, ins["xs"][j * 128:(j + 1) * 128, :], "ld")
        if c.on("vector"):
            c.wt("vector", "ld")
            c.eng.tensor_reduce(sb["tmp"][:, 0:1], xt, AX.X, ALU.add)
            c.eng.tensor_scalar_mul(sb["tmp"][:, 0:1], sb["tmp"][:, 0:1],
                                    1.0 / DM)
            c.eng.tensor_scalar(xn, xt, sb["tmp"][:, 0:1], None, ALU.subtract,
                                ALU.bypass)
        c.sig("vector", "vF")
        c.sig("vector", "vG")
        if c.on("scalar"):
            c.wt("scalar", "vG")
            c.eng.activation(c.ar(3072, DM), xn, ACT.Square, bias=0.0, scale=1.0,
                             accum_out=sb["tmp"][:, 1:2])
            c.ss("scalar", "ps")
            c.eng.activation(sb["tmp"][:, 2:3], sb["tmp"][:, 1:2], ACT.Sqrt,
                             bias=EPS, scale=1.0 / DM)
        c.sig("scalar", "sC")
        if c.on("vector"):
            c.wt("vector", "sC")
            c.eng.reciprocal(sb["tmp"][:, 6:7], sb["tmp"][:, 2:3])
            c.ss("vector", "pv")
            c.eng.tensor_scalar(xn, xn, sb["tmp"][:, 6:7], None, ALU.mult,
                                ALU.bypass)
            c.eng.tensor_reduce(sb["tmp"][:, 3:4], xn, AX.X, ALU.max,
                                apply_absolute_value=True)
            c.eng.tensor_scalar_max(sb["tmp"][:, 3:4], sb["tmp"][:, 3:4], EPS)
            c.eng.reciprocal(sb["tmp"][:, 4:5], sb["tmp"][:, 3:4])
            c.eng.tensor_scalar_mul(sb["tmp"][:, 5:6], sb["tmp"][:, 4:5], 127.0)
            c.eng.reciprocal(sb["sti"][:, j:j + 1], sb["tmp"][:, 5:6])
            v = c.ar(4096, DM)
            c.ss("vector", "pv")
            c.eng.tensor_scalar(v, xn, sb["tmp"][:, 5:6], None, ALU.mult,
                                ALU.bypass)
            k0 = sb["ki"][:, 0:DM]
            c.eng.tensor_copy(k0, v)
            d = c.ar(5120, DM)
            c.eng.tensor_tensor(d, v, k0, ALU.subtract)
            p_ = c.ar(2048, DM)
            c.eng.tensor_scalar(p_, d, 0.5, None, ALU.is_gt, ALU.bypass)
            c.eng.scalar_tensor_tensor(d, d, -0.5, p_, ALU.is_lt, ALU.subtract)
            xqt = c.ar(0, DM, BF16)
            c.eng.tensor_tensor(xqt, k0, d, ALU.subtract)
        c.sig("vector", "vH")
        if c.on("tensor"):
            c.wt("tensor", "vH")
            c.wt("tensor", "vI")
            c.wt("tensor", "sD")
            for f in range(8):
                bank = ps[f // 4]
                c.eng.transpose(bank[:, (f % 4) * 128:(f % 4) * 128 + 128],
                                xn[:, f * 128:(f + 1) * 128], sb["iden"][:])
            pb = ps[2][:].bitcast(BF16)
            for f in range(8):
                c.eng.transpose(pb[:, f * 128:(f + 1) * 128],
                                c.ar(0, DM, BF16)[:, f * 128:(f + 1) * 128],
                                sb["idenb"][:])
        c.sig("tensor", "tC")
        c.sig("tensor", "tD")
        if c.on("scalar"):
            c.wt("scalar", "tD")
            pb = ps[2][:].bitcast(BF16)
            for f in range(8):
                c.eng.copy(xqf[:, f * TSLAB + j * 128: f * TSLAB + (j + 1) * 128],
                           pb[:, f * 128:(f + 1) * 128])
        c.sig("scalar", "sD")
        if c.on("vector"):
            c.wt("vector", "tC")
            c.wt("vector", "gd")
            for f in range(8):
                bank = ps[f // 4]
                c.eng.tensor_copy(c.ar(1024, 1024)[:, f * 128:(f + 1) * 128],
                                  bank[:, (f % 4) * 128:(f % 4) * 128 + 128])
        c.sig("vector", "vI")
        for h in range(2):
            c.wt("gpsimd", "vI")
            c.dma("gpsimd",
                  c.dram["xnfm"][h * 512:(h + 1) * 512,
                                 j * 128:(j + 1) * 128].rearrange(
                      "(a p) q -> p a q", p=128),
                  c.ar(1024, 1024)[:, h * 512:(h + 1) * 512].rearrange(
                      "p (a q) -> p a q", a=4), "gd")
    c.barrier()


def ph_ffn1(c: Ctx):
    sb, ps, ins = c.sb, c.ps, c.ins
    srep = c.ar(0, 2052)
    stg = [c.ar(2052, 512), c.ar(2564, 512)]
    g01 = c.ar(3076, 1024)
    if c.on("tensor"):
        c.eng.transpose(ps[0][0:NTT, 0:128], sb["sti"][:, 0:NTT], sb["iden"][:])
    c.sig("tensor", "tE")
    if c.on("vector"):
        c.wt("vector", "tE")
        c.eng.tensor_copy(sb["sqt"][0:NTT, 0:128], ps[0][0:NTT, 0:128])
    c.sig("vector", "vJ0")
    c.wt("sync", "vJ0")
    c.dma("sync", sb["srow"][0:1, :], sb["sqt"][0:NTT, 0:128], "ld")
    if c.on("vector"):
        c.wt("vector", "ld")
        c.eng.tensor_scalar(sb["srow"][0:1, :], sb["srow"][0:1, :],
                            sb["sw"][0:1, 0:1], None, ALU.mult, ALU.bypass)
    c.sig("vector", "vJ")
    for ci, (c0, L) in enumerate(FF_CHUNKS):
        bank = ps[4 + ci % 2]
        if c.on("tensor"):
            c.wt("tensor", "vJ")
            c.wt("tensor", "vK", lag=1)
            c.eng.matmul(bank[:, 0:L], sb["onesr"][0:1, :],
                         sb["srow"][0:1, FF_OFF + c0:FF_OFF + c0 + L],
                         start=True, stop=True)
        c.sig("tensor", "tF")
        if c.on("vector"):
            c.wt("vector", "tF")
            c.eng.tensor_copy(srep[:, c0:c0 + L], bank[:, 0:L])
        c.sig("vector", "vK")

    for m in range(32):
        slot = m % 2
        wsl = sb["wsl"][:, slot * 1024:(slot + 1) * 1024]
        c.wt("sync", "sE", lag=1)
        c.dma("sync", wsl.rearrange("p (k m) -> p k m", k=8),
              ins["w1T"][:, m * 128:(m + 1) * 128].rearrange(
                  "(k p) m -> p k m", p=128), "ld")
        if c.on("scalar"):
            c.wt("scalar", "ld")
            c.wt("scalar", "vL")
            c.eng.activation(sb["sqt"][:, 0:1024], wsl, ACT.Abs, bias=0.0,
                             scale=sb["swc"][:, 2:3])
            c.eng.sign(sb["sqt2"][:, 0:1024], wsl)
        c.sig("scalar", "sE")
        if c.on("vector"):
            c.wt("vector", "sE")
            c.eng.tensor_scalar(g01, sb["sqt"][:, 0:1024], 0.5, None, ALU.is_gt,
                                ALU.bypass)
            c.eng.tensor_tensor(sb["wq"][:, slot * 1024:(slot + 1) * 1024],
                                g01, sb["sqt2"][:, 0:1024], ALU.mult)
        c.sig("vector", "vL")
        for ci, (c0, L) in enumerate(FF_CHUNKS):
            bank = ps[(m * len(FF_CHUNKS) + ci) % 2]
            wq = sb["wq"][:, slot * 1024:(slot + 1) * 1024]
            if c.on("tensor"):
                c.wt("tensor", "vL")
                c.wt("tensor", "vM", lag=1)
                for k in range(8):
                    c.eng.matmul(bank[:, 0:L], wq[:, k * 128:(k + 1) * 128],
                                 xqf_slice(c, k, FF_OFF + c0, L),
                                 start=(k == 0), stop=(k == 7))
            c.sig("tensor", "tG")
            if c.on("vector"):
                c.wt("vector", "tG")
                c.wt("vector", "gd", lag=16)
                c.eng.tensor_tensor(stg[ci % 2][:, 0:L], bank[:, 0:L],
                                    srep[:, c0:c0 + L], ALU.mult)
            c.sig("vector", "vM")
            c.wt("gpsimd", "vM")
            c.dma("gpsimd", c.dram["h1"][m * 128:(m + 1) * 128, c0:c0 + L],
                  stg[ci % 2][:, 0:L], "gd")
    c.barrier()


def xqf_slice(c, k, c0, L):
    return c.ar(18944, 8 * TSLAB, BF16)[:, k * TSLAB + c0:k * TSLAB + c0 + L]


def ph_ffn2(c: Ctx):
    sb = c.sb
    hmax = c.ar(4096, 2048)
    rows_ = [c.ar(0, 2052), c.ar(6144, 2052)]
    sils = [c.ar(8448, 2048), c.ar(10496, 2048)]
    abss = [c.ar(12544, 2048), c.ar(14592, 2048)]
    accs = [c.ar(16640, 2048), c.ar(18688, 2048)]
    acc2s = [c.ar(20736, 2048), c.ar(22784, 2048)]

    def vmax(g):
        if g < 0 or not c.on("vector"):
            return
        a = abss[g % 2][:, 0:TOK]
        if g == 0:
            c.eng.tensor_copy(hmax, a)
        else:
            c.eng.tensor_max(hmax, hmax, a)

    for g in range(32):
        s = g % 2
        row, sil = rows_[s], sils[s]
        c.wt("sync", "vN", lag=1)
        c.dma("sync", row, c.dram["h1"][g * 128:(g + 1) * 128, :], "ld")
        if g >= 2:
            if c.on("vector"):
                c.wt("vector", "sF2", lag=1)
            vmax(g - 2)
            c.sig("vector", "vO")
        if c.on("vector"):
            c.wt("vector", "ld")
            c.wt("vector", "sF", lag=1)
            acc = accs[s][:, 0:TOK]
            acc2 = acc2s[s][:, 0:TOK]
            c.eng.tensor_scalar(acc, row[:, 1:1 + TOK], sb["fcw"][:, g * 3:g * 3 + 1],
                                None, ALU.mult, ALU.bypass)
            c.eng.scalar_tensor_tensor(acc2, row[:, 2:2 + TOK],
                                       sb["fcw"][:, g * 3 + 1:g * 3 + 2], acc,
                                       ALU.mult, ALU.add)
            c.eng.scalar_tensor_tensor(acc, row[:, 3:3 + TOK],
                                       sb["fcw"][:, g * 3 + 2:g * 3 + 3], acc2,
                                       ALU.mult, ALU.add)
        c.sig("vector", "vN")
        if c.on("scalar"):
            c.wt("scalar", "vN")
            c.wt("scalar", "vO")
            c.wt("scalar", "gd", lag=16)
            c.eng.activation(sil, accs[s][:, 0:TOK], ACT.Silu,
                             bias=sb["fcb"][:, g:g + 1], scale=1.0)
        c.sig("scalar", "sF")
        if c.on("scalar"):
            c.eng.activation(abss[s][:, 0:TOK], sil, ACT.Abs, bias=0.0,
                             scale=1.0)
        c.sig("scalar", "sF2")
        c.wt("gpsimd", "sF")
        c.dma("gpsimd", c.dram["h1"][g * 128:(g + 1) * 128, 0:TOK], sil, "gd")
    if c.on("vector"):
        c.wt("vector", "sF2")
    vmax(30)
    vmax(31)
    c.barrier()


def ph_ffn3(c: Ctx):
    sb, ps, ins = c.sb, c.ps, c.ins
    hmax = c.ar(4096, 2048)
    srep2 = c.ar(2048, 2048)
    p_ = c.ar(0, 2048)
    for k in range(16):
        bank = ps[k % 2]
        if c.on("tensor"):
            c.wt("tensor", "vP", lag=1)
            c.eng.transpose(bank[:, 0:128], hmax[:, k * 128:(k + 1) * 128],
                            sb["iden"][:])
        c.sig("tensor", "tH")
        if c.on("vector"):
            c.wt("vector", "tH")
            c.eng.tensor_reduce(sb["q2"][:, k:k + 1], bank[:, 0:128], AX.X,
                                ALU.max)
        c.sig("vector", "vP")
    if c.on("vector"):
        c.eng.tensor_scalar_max(sb["q2"][:, 0:16], sb["q2"][:, 0:16], EPS)
        c.eng.reciprocal(sb["q2"][:, 16:32], sb["q2"][:, 0:16])
        c.eng.tensor_scalar_mul(sb["q2"][:, 32:48], sb["q2"][:, 16:32], 127.0)
        c.eng.reciprocal(sb["q2"][:, 48:64], sb["q2"][:, 32:48])
        c.eng.tensor_scalar(sb["q2"][:, 64:80], sb["q2"][:, 48:64],
                            sb["swc"][:, 1:2], None, ALU.mult, ALU.bypass)
    c.sig("vector", "vR")
    if c.on("tensor"):
        c.wt("tensor", "vR")
        c.eng.transpose(ps[0][0:16, 0:128], sb["q2"][:, 32:48], sb["iden"][:])
    c.sig("tensor", "tI")
    if c.on("vector"):
        c.wt("vector", "tI")
        c.eng.tensor_copy(sb["sqt"][0:16, 0:128], ps[0][0:16, 0:128])
    c.sig("vector", "vR0")
    c.wt("sync", "vR0")
    c.dma("sync", sb["srow"][0:1, 0:TOK], sb["sqt"][0:16, 0:128], "ld")
    for ci, (c0, L) in enumerate(OWN_CHUNKS):
        bank = ps[ci % 2]
        if c.on("tensor"):
            c.wt("tensor", "ld")
            c.wt("tensor", "vS", lag=1)
            c.eng.matmul(bank[:, 0:L], sb["onesr"][0:1, :],
                         sb["srow"][0:1, c0:c0 + L], start=True, stop=True)
        c.sig("tensor", "tJ")
        if c.on("vector"):
            c.wt("vector", "tJ")
            c.eng.tensor_copy(srep2[:, c0:c0 + L], bank[:, 0:L])
        c.sig("vector", "vS")

    rows_ = [c.ar(22528, 2048), c.ar(24576, 2048)]
    qstg = [c.ar(26624, 2048, BF16), c.ar(27648, 2048, BF16)]
    for g in range(32):
        s = g % 2
        row = rows_[s]
        c.wt("sync", "vT", lag=1)
        c.dma("sync", row, c.dram["h1"][g * 128:(g + 1) * 128, 0:TOK], "ld")
        if c.on("vector"):
            c.wt("vector", "ld")
            c.wt("vector", "gd", lag=16)
            v = sb["sqt"][:, 0:TOK]
            c.eng.tensor_tensor(v, row, srep2, ALU.mult)
            k0 = sb["ki"][:, 0:TOK]
            c.eng.tensor_copy(k0, v)
            d = sb["sqt2"][:, 0:TOK]
            c.eng.tensor_tensor(d, v, k0, ALU.subtract)
            c.eng.tensor_scalar(p_, d, 0.5, None, ALU.is_gt, ALU.bypass)
            c.eng.scalar_tensor_tensor(d, d, -0.5, p_, ALU.is_lt, ALU.subtract)
            c.eng.tensor_tensor(qstg[s][:, 0:2048], k0, d, ALU.subtract)
        c.sig("vector", "vT")
        c.wt("gpsimd", "vT")
        c.dma("gpsimd", c.dram["hq2"][g * 128:(g + 1) * 128, :],
              qstg[s][:, 0:2048], "gd")
    for k in range(32):
        slot = k % 2
        wsl = sb["wsl"][:, slot * 1024:(slot + 1) * 1024]
        c.wt("sync", "sH", lag=1)
        c.dma("sync", wsl, ins["w2T"][k * 128:(k + 1) * 128, :], "ld")
        if c.on("scalar"):
            c.wt("scalar", "ld")
            c.wt("scalar", "vU")
            c.wt("scalar", "vT")
            c.eng.activation(sb["sqt"][:, 0:1024], wsl, ACT.Abs, bias=0.0,
                             scale=sb["swc"][:, 3:4])
            c.eng.sign(sb["sqt2"][:, 0:1024], wsl)
        c.sig("scalar", "sH")
        if c.on("vector"):
            c.wt("vector", "sH")
            c.eng.tensor_scalar(p_[:, 0:1024], sb["sqt"][:, 0:1024], 0.5, None,
                                ALU.is_gt, ALU.bypass)
            c.eng.tensor_tensor(c.ar(6144, 32768, BF16)[:, k * 1024:(k + 1) * 1024],
                                p_[:, 0:1024], sb["sqt2"][:, 0:1024], ALU.mult)
        c.sig("vector", "vU")
    c.barrier()


def ph_ffn4(c: Ctx):
    sb, ps, ins = c.sb, c.ps, c.ins
    w2q = c.ar(6144, 32768, BF16)
    lhs = c.ar(0, 4096, BF16)
    stg = [c.ar(2048, 512), c.ar(2560, 512)]
    for tt in range(16):
        xrow = sb["row"][:, 0:1024]
        c.wt("sync", "vV", lag=0)
        c.dma("sync", xrow, ins["xs"][PRE + tt * 128:PRE + (tt + 1) * 128, :],
              "ld")
        for k in range(32):
            c.dma("sync", lhs[:, k * 128:(k + 1) * 128],
                  c.dram["hq2"][k * 128:(k + 1) * 128,
                                tt * 128:(tt + 1) * 128], "ld")
        for oc in range(2):
            bank = ps[oc]
            if c.on("tensor"):
                c.wt("tensor", "ld")
                c.wt("tensor", "vV", lag=1)
                for k in range(32):
                    c.eng.matmul(bank[:, 0:512], lhs[:, k * 128:(k + 1) * 128],
                                 w2q[:, k * 1024 + oc * 512:
                                     k * 1024 + oc * 512 + 512],
                                 start=(k == 0), stop=(k == 31))
            c.sig("tensor", "tK")
            if c.on("vector"):
                c.wt("vector", "tK")
                c.wt("vector", "gd", lag=16)
                c.eng.scalar_tensor_tensor(stg[oc][:, 0:512], bank[:, 0:512],
                                           sb["q2"][:, 64 + tt:65 + tt],
                                           xrow[:, oc * 512:(oc + 1) * 512],
                                           ALU.mult, ALU.add)
            c.sig("vector", "vV")
            c.wt("gpsimd", "vV")
            c.dma("gpsimd", c.dram["fout"][tt * 128:(tt + 1) * 128,
                                           oc * 512:(oc + 1) * 512],
                  stg[oc][:, 0:512], "gd")
    c.barrier()


def ph_win(c: Ctx):
    sb, ps, ins = c.sb, c.ps, c.ins
    bigx = c.ar(0, 19456)
    mrep = c.ar(19456, 2304)
    stg = [c.ar(21760, 512), c.ar(22272, 512)]
    for f in range(8):
        c.dma("sync", bigx[:, f * TSLAB:(f + 1) * TSLAB],
              c.dram["xnfm"][f * 128:(f + 1) * 128, :], "ld")
    for ci, (c0, L) in enumerate(SC_CHUNKS):
        bank = ps[2 + ci % 2]
        if c.on("tensor"):
            c.wt("tensor", "ld")
            c.wt("tensor", "vW", lag=1)
            c.eng.matmul(bank[:, 0:L], sb["onesr"][0:1, :],
                         sb["msk"][0:1, c0:c0 + L], start=True, stop=True)
        c.sig("tensor", "tL")
        if c.on("vector"):
            c.wt("vector", "tL")
            c.eng.tensor_copy(mrep[:, c0:c0 + L], bank[:, 0:L])
        c.sig("vector", "vW")

    for m in range(32):
        slot = m % 2
        wsl = sb["wsl"][:, slot * 1024:(slot + 1) * 1024]
        c.wt("sync", "tM", lag=1)
        c.dma("sync", wsl.rearrange("p (k m) -> p k m", k=8),
              ins["w_inT"][:, m * 128:(m + 1) * 128].rearrange(
                  "(k p) m -> p k m", p=128), "ld")
        chunks = XIN_CHUNKS if m < 16 else OWN_CHUNKS
        base = XIN_OFF if m < 16 else PRE
        for ci, (c0, L) in enumerate(chunks):
            bank = ps[ci % 2]
            if c.on("tensor"):
                c.wt("tensor", "ld")
                c.wt("tensor", "eT", lag=1)
                for k in range(8):
                    c.eng.matmul(bank[:, 0:L], wsl[:, k * 128:(k + 1) * 128],
                                 bigx[:, k * TSLAB + base + c0:
                                      k * TSLAB + base + c0 + L],
                                 start=(k == 0), stop=(k == 7))
            c.sig("tensor", "tN")
            if m < 16:
                if c.on("vector"):
                    c.wt("vector", "tN")
                    c.eng.tensor_copy(sb["row"][:, c0:c0 + L], bank[:, 0:L])
                c.sig("vector", "eT")
            else:
                if c.on("scalar"):
                    c.wt("scalar", "tN")
                    c.wt("scalar", "gd", lag=16)
                    c.eng.activation(stg[ci % 2][:, 0:L], bank[:, 0:L], ACT.Silu,
                                     bias=0.0, scale=1.0)
                c.sig("scalar", "eT")
                c.sig("scalar", "sI")
                c.wt("gpsimd", "sI")
                c.dma("gpsimd",
                      c.dram["sz"][(m - 16) * 128:(m - 15) * 128, c0:c0 + L],
                      stg[ci % 2][:, 0:L], "gd")
        if m < 16:
            for ci, (c0, L) in enumerate(SC_CHUNKS):
                if c.on("vector"):
                    c.wt("vector", "sI")
                    acc = sb["sqt"][:, 0:512]
                    acc2 = sb["sqt2"][:, 0:512]
                    row = sb["row"]
                    c.eng.tensor_scalar(acc[:, 0:L], row[:, c0:c0 + L],
                                        sb["cvw"][:, m * 4:m * 4 + 1], None,
                                        ALU.mult, ALU.bypass)
                    c.eng.scalar_tensor_tensor(
                        acc2[:, 0:L], row[:, c0 + 1:c0 + 1 + L],
                        sb["cvw"][:, m * 4 + 1:m * 4 + 2], acc[:, 0:L],
                        ALU.mult, ALU.add)
                    c.eng.scalar_tensor_tensor(
                        acc[:, 0:L], row[:, c0 + 2:c0 + 2 + L],
                        sb["cvw"][:, m * 4 + 2:m * 4 + 3], acc2[:, 0:L],
                        ALU.mult, ALU.add)
                    c.eng.scalar_tensor_tensor(
                        acc2[:, 0:L], row[:, c0 + 3:c0 + 3 + L],
                        sb["cvw"][:, m * 4 + 3:m * 4 + 4], acc[:, 0:L],
                        ALU.mult, ALU.add)
                    c.eng.tensor_tensor(acc[:, 0:L], acc2[:, 0:L],
                                        mrep[:, c0:c0 + L], ALU.mult)
                c.sig("vector", "vX")
                if c.on("scalar"):
                    c.wt("scalar", "vX")
                    c.wt("scalar", "gd", lag=16)
                    c.eng.activation(stg[ci % 2][:, 0:L], sb["sqt"][:, 0:L],
                                     ACT.Silu, bias=sb["cvb"][:, m:m + 1],
                                     scale=1.0)
                c.sig("scalar", "sI")
                c.wt("gpsimd", "sI")
                c.dma("gpsimd",
                      c.dram["xconv"][m * 128:(m + 1) * 128, c0:c0 + L],
                      stg[ci % 2][:, 0:L], "gd")
            c.sig("tensor", "tM")
        else:
            c.sig("tensor", "tM")
    c.barrier()


DT_CHUNKS = [(o, 256) for o in range(0, SC_LEN, 256)]


def dt_tiles_load(c: Ctx, c0, L):
    for k in range(16):
        c.wt("sync", "vY")
        c.dma("sync", c.ar(24576 + k * 256, 256)[:, 0:L],
              c.dram["xconv"][k * 128:(k + 1) * 128, c0:c0 + L], "ld")


def dt_m_slice(c: Ctx, c0, L, m):
    sb, ps, ins = c.sb, c.ps, c.ins
    wqf = sb["wq"][:].bitcast(F32)
    stg = [wqf[:, 0:512], wqf[:, 512:1024]]
    slot = m % 2
    wsl = sb["wsl"][:, slot * 2048:slot * 2048 + 2048]
    c.wt("sync", "tO", lag=1)
    c.dma("sync", wsl.rearrange("p (k m) -> p k m", k=16),
          ins["w_dtT"][:, m * 128:(m + 1) * 128].rearrange(
              "(k p) m -> p k m", p=128), "ld")
    bank = ps[m % 2]
    if c.on("tensor"):
        c.wt("tensor", "ld")
        c.wt("tensor", "eU", lag=1)
        for k in range(16):
            c.eng.matmul(bank[:, 0:L], wsl[:, k * 128:(k + 1) * 128],
                         c.ar(24576 + k * 256, 256)[:, 0:L],
                         start=(k == 0), stop=(k == 15))
    c.sig("tensor", "tO")
    c.sig("tensor", "tP")
    if c.on("scalar"):
        c.wt("scalar", "tP")
        c.wt("scalar", "gd", lag=48)
        c.eng.activation(sb["sqt"][:, 0:L], bank[:, 0:L], ACT.Exp,
                         bias=sb["dtb"][:, m:m + 1], scale=1.0)
        c.eng.activation(stg[m % 2][:, 0:L], sb["sqt"][:, 0:L], ACT.Ln,
                         bias=1.0, scale=1.0)
    c.sig("scalar", "eU")
    c.sig("scalar", "sJ")
    if c.on("vector"):
        c.wt("vector", "sJ")
        c.wt("vector", "gd", lag=32)
        c.eng.tensor_tensor(
            sb["sqt2"][:, (m % 2) * 512:(m % 2) * 512 + L],
            stg[m % 2][:, 0:L],
            c.ar(24576 + m * 256, 256)[:, 0:L], ALU.mult)
    c.sig("vector", "vZ")
    c.wt("gpsimd", "vZ")
    c.dma("gpsimd", c.dram["dt"][m * 128:(m + 1) * 128, c0:c0 + L],
          stg[m % 2][:, 0:L], "gd")
    c.dma("gpsimd", c.dram["w"][m * 128:(m + 1) * 128, c0:c0 + L],
          sb["sqt2"][:, (m % 2) * 512:(m % 2) * 512 + L], "gd")


def dt_bc_piece(c: Ctx, c0, L):
    sb, ps = c.sb, c.ps
    bank = ps[2]
    if c.on("tensor"):
        c.wt("tensor", "vAA", lag=1)
        for k in range(16):
            c.eng.matmul(bank[0:32, 0:L], sb["wbc"][:, k * 32:(k + 1) * 32],
                         c.ar(24576 + k * 256, 256)[:, 0:L],
                         start=(k == 0), stop=(k == 15))
    c.sig("tensor", "tQ")
    if c.on("vector"):
        c.wt("vector", "tQ")
        c.eng.tensor_copy(sb["bc"][0:32, c0:c0 + L], bank[0:32, 0:L])
    c.sig("vector", "vAA")
    c.sig("vector", "vY")


def scan_reps(c: Ctx, s0, SL, vaa_snap):
    sb, ps, ins = c.sb, c.ps, c.ins
    reps = c.ar(0, 16384)
    for jj in range(32):
        bank = ps[4 + jj % 2]
        oh = sb["oh"][:, (jj % 2) * 128:(jj % 2) * 128 + 128]
        c.wt("sync", "tR", lag=1)
        c.dma("sync", oh, ins["ohc"][:, jj * 128:(jj + 1) * 128], "ld")
        if c.on("tensor"):
            c.wt_abs("tensor", "vAA", vaa_snap)
            c.wt("tensor", "ld")
            c.wt("tensor", "vAD", lag=1)
            c.eng.matmul(bank[:, 0:SL], oh, sb["bc"][0:32, s0:s0 + SL],
                         start=True, stop=True)
        c.sig("tensor", "tR")
        if c.on("vector"):
            if jj == 0:
                c.wt("vector", "gA")
            c.wt("vector", "tR")
            c.eng.tensor_copy(reps[:, jj * 512:jj * 512 + SL], bank[:, 0:SL])
        c.sig("vector", "vAD")


def scan_dg_piece(c: Ctx, s0, SL, dg, gd_snap):
    sb, ps = c.sb, c.ps
    own0 = max(OWN_SC - s0, 0)
    Lo = SL - own0
    reps = c.ar(0, 16384)
    h3f = c.ar(16384, 8192)
    kif = sb["ki"][:].bitcast(F32)
    slot = dg % 2
    dtr = kif[:, slot * 1024:slot * 1024 + SL]
    wr = kif[:, slot * 1024 + 512:slot * 1024 + 512 + SL]
    c.wt_abs("sync", "gd", gd_snap)
    c.wt("sync", "dwf", lag=1)
    c.dma("sync", dtr, c.dram["dt"][dg * 128:(dg + 1) * 128, s0:s0 + SL], "ld")
    c.dma("sync", wr, c.dram["w"][dg * 128:(dg + 1) * 128, s0:s0 + SL], "ld")

    def h3b(n):
        return h3f[:, n * 512:n * 512 + SL]

    GG = 4
    sqf = [sb["row"][:, 0:2048],
           None]
    for g in range(NST // GG):

        def dab(j):
            if g % 2 == 0:
                return sb["row"][:, j * 512:j * 512 + SL]
            t = [sb["sqt"], sb["sqt"], sb["sqt2"], sb["sqt2"]][j]
            o = [1024, 1536, 1024, 1536][j]
            return t[:, o:o + SL]
        if c.on("scalar"):
            c.wt("scalar", "ld")
            c.wt("scalar", "vAE", lag=2)
            for j in range(GG):
                n = g * GG + j
                c.eng.activation(dab(j), dtr, ACT.Exp, bias=0.0,
                                 scale=sb["A"][:, dg * 16 + n:dg * 16 + n + 1])
        c.sig("scalar", "sK")
        if c.on("vector"):
            c.wt("vector", "ld")
            c.wt("vector", "sK")
            for j in range(GG):
                n = g * GG + j
                c.eng.tensor_tensor(h3b(n), wr,
                                    reps[:, n * 512:n * 512 + SL], ALU.mult)
            for j in range(GG):
                n = g * GG + j
                c.eng.tensor_tensor_scan(
                    h3b(n), dab(j), h3b(n),
                    sb["state"][:, dg * 16 + n:dg * 16 + n + 1],
                    ALU.mult, ALU.add)
            for j in range(GG):
                n = g * GG + j
                c.eng.tensor_copy(
                    sb["state"][:, dg * 16 + n:dg * 16 + n + 1],
                    h3f[:, n * 512 + SL - 1:n * 512 + SL])
        c.sig("vector", "vAE")
        if c.on("gpsimd"):
            c.wt("gpsimd", "vAE")
            for j in range(GG):
                n = g * GG + j
                c.eng.tensor_tensor(
                    h3b(n)[:, own0:SL], h3b(n)[:, own0:SL],
                    reps[:, (16 + n) * 512 + own0:(16 + n) * 512 + SL],
                    ALU.mult)
        c.sig("gpsimd", "gA")
    c.sig("vector", "dwf")
    if c.on("gpsimd"):
        for j in range(8):
            c.eng.tensor_tensor(h3b(2 * j)[:, own0:SL], h3b(2 * j)[:, own0:SL],
                                h3b(2 * j + 1)[:, own0:SL], ALU.add)
    c.sig("gpsimd", "gB")
    if c.on("vector"):
        c.wt("vector", "gB")
        c.wt("vector", "ld")
        for j in range(4):
            c.eng.tensor_tensor(h3b(4 * j)[:, own0:SL], h3b(4 * j)[:, own0:SL],
                                h3b(4 * j + 2)[:, own0:SL], ALU.add)
        for j in range(2):
            c.eng.tensor_tensor(h3b(8 * j)[:, own0:SL], h3b(8 * j)[:, own0:SL],
                                h3b(8 * j + 4)[:, own0:SL], ALU.add)
        c.eng.tensor_tensor(sb["ystg"][:, 0:Lo], h3b(0)[:, own0:SL],
                            h3b(8)[:, own0:SL], ALU.add)
    c.sig("vector", "vAG")
    c.wt("sync", "vAG")
    c.dma("sync", c.dram["y"][dg * 128:(dg + 1) * 128,
                              s0 + own0 - OWN_SC:s0 + SL - OWN_SC],
          sb["ystg"][:, 0:Lo], "ld")


def ph_dtscan(c: Ctx):
    sb = c.sb
    if c.on("vector"):
        c.eng.memset(sb["state"][:], 0.0)
    snap_gd, snap_vaa = {}, {}

    def dt_halves(i):
        c0, L = SC_CHUNKS[i]
        for h0 in range(c0, c0 + L, 256):
            hL = min(256, c0 + L - h0)
            yield lambda h0=h0, hL=hL: dt_tiles_load(c, h0, hL)
            for m in range(16):
                yield (lambda h0=h0, hL=hL, m=m: dt_m_slice(c, h0, hL, m))
            yield lambda h0=h0, hL=hL: dt_bc_piece(c, h0, hL)

    def run_all(gen):
        for t in gen:
            t()

    nch = len(SC_CHUNKS)
    run_all(dt_halves(0))
    snap_gd[0] = c.cnt["gd"]
    snap_vaa[0] = c.cnt["vAA"]
    for i in range(nch):
        c0, L = SC_CHUNKS[i]
        scan_reps(c, c0, L, snap_vaa[i])
        if i + 1 < nch:
            thunks = list(dt_halves(i + 1))
            per = (len(thunks) + 15) // 16
            for dg in range(16):
                for t in thunks[dg * per:(dg + 1) * per]:
                    t()
                scan_dg_piece(c, c0, L, dg, snap_gd[i])
            snap_gd[i + 1] = c.cnt["gd"]
            snap_vaa[i + 1] = c.cnt["vAA"]
        else:
            for dg in range(16):
                scan_dg_piece(c, c0, L, dg, snap_gd[i])
    c.barrier()


def ph_out(c: Ctx):
    sb, ps, ins = c.sb, c.ps, c.ins
    woutT = c.ar(0, 16384)
    gts = [c.ar(16384, 2048), c.ar(18432, 2048)]
    xrow = sb["row"][:, 0:1024]
    stg = [sb["row"][:, 1024:1536], sb["row"][:, 1536:2048]]
    for k in range(16):
        c.dma("sync", woutT[:, k * 1024:(k + 1) * 1024],
              ins["w_outT"][k * 128:(k + 1) * 128, :], "ld")
    for tt in range(16):
        slot = tt % 2
        ya = c.ar(20480 + slot * 4096, 2048)
        za = c.ar(20480 + slot * 4096 + 2048, 2048)
        gt = gts[slot]
        c.wt("sync", "vAH", lag=1)
        c.dma("sync", ya.rearrange("p (k q) -> p k q", k=16),
              c.dram["y"][:, tt * 128:(tt + 1) * 128].rearrange(
                  "(k p) q -> p k q", p=128), "ld")
        c.wt("gpsimd", "vAH", lag=1)
        c.dma("gpsimd", za.rearrange("p (k q) -> p k q", k=16),
              c.dram["sz"][:, tt * 128:(tt + 1) * 128].rearrange(
                  "(k p) q -> p k q", p=128), "gd")
        c.wt("sync", "vAJ")
        c.dma("sync", xrow, c.dram["fout"][tt * 128:(tt + 1) * 128, :], "ld")
        if c.on("vector"):
            c.wt("vector", "ld")
            c.wt("vector", "gd")
            c.wt("vector", "tS", lag=1)
            c.eng.tensor_tensor(gt[:], ya, za, ALU.mult)
        c.sig("vector", "vAH")
        c.sig("vector", "vAI")
        for oc in range(2):
            bank = ps[oc]
            if c.on("tensor"):
                c.wt("tensor", "vAI")
                c.wt("tensor", "eV", lag=1)
                for k in range(16):
                    c.eng.matmul(bank[:, 0:512], gt[:, k * 128:(k + 1) * 128],
                                 woutT[:, k * 1024 + oc * 512:
                                       k * 1024 + oc * 512 + 512],
                                 start=(k == 0), stop=(k == 15))
            c.sig("tensor", "tT")
            if c.on("vector"):
                c.wt("vector", "ld")
                c.wt("vector", "tT")
                c.wt("vector", "gd", lag=32)
                c.eng.tensor_tensor(stg[oc][:, 0:512], bank[:, 0:512],
                                    xrow[:, oc * 512:(oc + 1) * 512], ALU.add)
            c.sig("vector", "eV")
            c.sig("vector", "vAJ")
            c.wt("gpsimd", "vAJ")
            c.dma("gpsimd", c.out[tt * 128:(tt + 1) * 128,
                                  oc * 512:(oc + 1) * 512],
                  stg[oc][:, 0:512], "gd")
        c.sig("tensor", "tS")
    if c.on("gpsimd"):
        c.wt("gpsimd", "gd")
    c.barrier()


def _tail_debug(c: Ctx):
    c.dma("gpsimd", c.out[:], c.ins["xs"][0:TOK, :], "gd")
    if c.on("gpsimd"):
        c.wt("gpsimd", "gd")
    c.barrier()


N_PHASES = 10
DBG_DRAM = False
DBG_RACE = False


def emit_all(c: Ctx):
    phases = [ph_consts, ph_ln, ph_ffn1, ph_ffn2, ph_ffn3, ph_ffn4, ph_win,
              ph_dtscan, ph_out]
    for ph in phases[:N_PHASES]:
        ph(c)
    if N_PHASES < 10:
        _tail_debug(c)


SEM_NAMES = [
    "bar", "ld", "gd", "sA", "vA", "vB", "vC", "tA", "vD", "sB", "tB", "vE",
    "vF", "vG", "sC", "vH", "tC", "tD", "sD", "vI", "qA",
    "tE", "vJ0", "vJ", "tF", "vK", "vL", "sE", "tG", "vM", "qB",
    "qC", "vN", "sF", "sF2", "vO",
    "vP", "tH", "vQ", "sG", "vR", "vR0", "tI", "tJ", "vS", "vT", "qD", "vU", "sH",
    "vV", "tK", "qE",
    "vW", "tL", "tM", "eT", "tN", "sI", "qF", "vX",
    "vY", "tO", "tP", "eU", "sJ", "vZ", "qG", "tQ", "vAA",
    "vAB", "vAC", "tR", "vAD", "dwf", "sK", "vAE", "vAF", "gA", "gB", "vAG",
    "vAH", "vAI", "tS", "eV", "tT", "vAJ", "qH", "pv", "ps",
    "au_sync", "au_tensor", "au_vector", "au_scalar", "au_gpsimd",
]


def build_nc():
    nc = bass.Bass("TRN2", target_bir_lowering=False,
                   detect_race_conditions=DBG_RACE)
    ins = {
        "xs": nc.dram_tensor("xs", [TSLAB, DM], F32, kind="ExternalInput").ap(),
        "msk": nc.dram_tensor("msk", [1, SC_LEN], F32, kind="ExternalInput").ap(),
        "w_inT": nc.dram_tensor("w_inT", [DM, 2 * DI], F32, kind="ExternalInput").ap(),
        "w_dtT": nc.dram_tensor("w_dtT", [DI, DI], F32, kind="ExternalInput").ap(),
        "w_bcT": nc.dram_tensor("w_bcT", [128, 512], F32, kind="ExternalInput").ap(),
        "w_outT": nc.dram_tensor("w_outT", [DI, DM], F32, kind="ExternalInput").ap(),
        "w1T": nc.dram_tensor("w1T", [DM, DH], F32, kind="ExternalInput").ap(),
        "w2T": nc.dram_tensor("w2T", [DH, DM], F32, kind="ExternalInput").ap(),
        "alog": nc.dram_tensor("alog", [128, 256], F32, kind="ExternalInput").ap(),
        "cvw": nc.dram_tensor("cvw", [128, 64], F32, kind="ExternalInput").ap(),
        "cvb": nc.dram_tensor("cvb", [128, 16], F32, kind="ExternalInput").ap(),
        "dtb": nc.dram_tensor("dtb", [128, 16], F32, kind="ExternalInput").ap(),
        "fcw": nc.dram_tensor("fcw", [128, 96], F32, kind="ExternalInput").ap(),
        "fcb": nc.dram_tensor("fcb", [128, 32], F32, kind="ExternalInput").ap(),
        "iden": nc.dram_tensor("iden", [128, 128], F32, kind="ExternalInput").ap(),
        "idenb": nc.dram_tensor("idenb", [128, 128], BF16, kind="ExternalInput").ap(),
        "onesr": nc.dram_tensor("onesr", [1, 128], F32, kind="ExternalInput").ap(),
        "onescol": nc.dram_tensor("onescol", [128, 1], F32, kind="ExternalInput").ap(),
        "kconst": nc.dram_tensor("kconst", [128, 4], F32, kind="ExternalInput").ap(),
        "ohc": nc.dram_tensor("ohc", [32, 4096], F32, kind="ExternalInput").ap(),
    }
    out = nc.dram_tensor("o", [TOK, DM], F32, kind="ExternalOutput").ap()
    dk = "ExternalOutput" if DBG_DRAM else "Internal"
    dram = {
        "xnfm": nc.dram_tensor("d_xnfm", [DM, TSLAB], F32, kind=dk).ap(),
        "xconv": nc.dram_tensor("d_xconv", [DI, SC_LEN], F32, kind=dk).ap(),
        "dt": nc.dram_tensor("d_dt", [DI, SC_LEN], F32, kind=dk).ap(),
        "w": nc.dram_tensor("d_w", [DI, SC_LEN], F32, kind=dk).ap(),
        "sz": nc.dram_tensor("d_sz", [DI, TOK], F32, kind=dk).ap(),
        "y": nc.dram_tensor("d_y", [DI, TOK], F32, kind=dk).ap(),
        "h1": nc.dram_tensor("d_h1", [DH, FF_LEN], F32, kind=dk).ap(),
        "hq2": nc.dram_tensor("d_hq2", [DH, TOK], BF16, kind=dk).ap(),
        "fout": nc.dram_tensor("d_fout", [TOK, DM], F32, kind=dk).ap(),
    }
    es = ExitStack()
    with es:
        def sbuf(name, shape, dt=F32):
            return es.enter_context(nc.sbuf_tensor(name, shape, dt))

        sb = {
            "alog": sbuf("s_alog", [128, 256]),
            "A": sbuf("s_A", [128, 256]),
            "cvw": sbuf("s_cvw", [128, 64]),
            "cvb": sbuf("s_cvb", [128, 16]),
            "dtb": sbuf("s_dtb", [128, 16]),
            "fcw": sbuf("s_fcw", [128, 96]),
            "fcb": sbuf("s_fcb", [128, 32]),
            "iden": sbuf("s_iden", [128, 128]),
            "idenb": sbuf("s_idenb", [128, 128], BF16),
            "onesr": sbuf("s_onesr", [1, 128]),
            "onescol": sbuf("s_onescol", [128, 1]),
            "msk": sbuf("s_msk", [1, SC_LEN]),
            "wbc": sbuf("s_wbc", [128, 512]),
            "bc": sbuf("s_bc", [32, SC_LEN]),
            "state": sbuf("s_state", [128, 256]),
            "sw": sbuf("s_sw", [1, 8]),
            "swc": sbuf("s_swc", [128, 8]),
            "acc": sbuf("s_acc", [128, 4]),
            "tmp": sbuf("s_tmp", [128, 8]),
            "sti": sbuf("s_sti", [128, NTT]),
            "q2": sbuf("s_q2", [128, 80]),
            "srow": sbuf("s_srow", [1, TSLAB]),
            "oh": sbuf("s_oh", [32, 256]),
            "kc": sbuf("s_kc", [128, 4]),
            "sqt": sbuf("s_sqt", [128, 2048]),
            "sqt2": sbuf("s_sqt2", [128, 2048]),
            "ki": sbuf("s_ki", [128, 2048], I32),
            "wsl": sbuf("s_wsl", [128, 4096]),
            "wq": sbuf("s_wq", [128, 2048], BF16),
            "row": sbuf("s_row", [128, 2308]),
            "ystg": sbuf("s_ystg", [128, 512]),
            "AR": sbuf("s_AR", [128, ARW]),
        }
        ps = [es.enter_context(nc.psum_tensor(f"ps{i}", [128, 512], F32))
              for i in range(8)]
        nc.const_aps.register(F32, EPS, sb["kc"][:, 0:1])
        sems = {s: es.enter_context(nc.semaphore(s)) for s in SEM_NAMES}

        with nc.Block() as block:
            def mk(en):
                def f(eng):
                    cc = Ctx(en, eng, sems, sb, ps, dram, ins, out)
                    emit_all(cc)
                return f
            block.sync(mk("sync"))
            block.tensor(mk("tensor"))
            block.vector(mk("vector"))
            block.scalar(mk("scalar"))
            block.gpsimd(mk("gpsimd"))
    return nc


def kernel(L0_attn_x, L0_attn_W_in, L0_attn_W_conv, L0_attn_b_conv,
           L0_attn_W_dt, L0_attn_b_dt, L0_attn_W_B, L0_attn_W_C, L0_attn_A_log,
           L0_attn_W_out, L0_ffn_w1, L0_ffn_conv_w, L0_ffn_conv_b, L0_ffn_w2):
    import ml_dtypes
    x = np.ascontiguousarray(np.asarray(L0_attn_x, dtype=np.float32)[0])
    T = x.shape[0]
    cont = np.ascontiguousarray

    def dgmaj(a):
        g = a.shape[0] // 128
        return cont(a.reshape(g, 128, a.shape[1]).transpose(1, 0, 2)
                     .reshape(128, g * a.shape[1]))

    shared = {
        "w_inT": cont(np.asarray(L0_attn_W_in, np.float32).T),
        "w_dtT": cont(np.asarray(L0_attn_W_dt, np.float32).T),
        "w_bcT": dgmaj(np.concatenate([np.asarray(L0_attn_W_B, np.float32),
                                       np.asarray(L0_attn_W_C, np.float32)],
                                      0).T),
        "w_outT": cont(np.asarray(L0_attn_W_out, np.float32).T),
        "w1T": cont(np.asarray(L0_ffn_w1, np.float32).T),
        "w2T": cont(np.asarray(L0_ffn_w2, np.float32).T),
        "alog": dgmaj(np.asarray(L0_attn_A_log, np.float32)),
        "cvw": dgmaj(np.asarray(L0_attn_W_conv, np.float32)[:, 0, :]),
        "cvb": dgmaj(np.asarray(L0_attn_b_conv, np.float32)[:, None]),
        "dtb": dgmaj(np.asarray(L0_attn_b_dt, np.float32)[:, None]),
        "fcw": dgmaj(np.asarray(L0_ffn_conv_w, np.float32)[:, 0, :]),
        "fcb": dgmaj(np.asarray(L0_ffn_conv_b, np.float32)[:, None]),
        "iden": np.eye(128, dtype=np.float32),
        "idenb": np.eye(128, dtype=ml_dtypes.bfloat16),
        "onesr": np.ones((1, 128), np.float32),
        "onescol": np.ones((128, 1), np.float32),
        "kconst": np.full((128, 4), EPS, np.float32),
        "ohc": np.kron(np.eye(32, dtype=np.float32),
                       np.ones((1, 128), np.float32)),
    }

    in_maps = []
    for i in range(NCORES):
        s0 = i * TOK
        lo, hi = s0 - PRE, s0 + TOK + POST
        pl, pr = max(0, -lo), max(0, hi - T)
        slab = np.pad(x[max(lo, 0):min(hi, T)], ((pl, pr), (0, 0)))
        msk = np.zeros((1, SC_LEN), np.float32)
        msk[0, max(0, -(s0 - 256)):] = 1.0
        m = dict(shared)
        m["xs"] = cont(slab)
        m["msk"] = msk
        in_maps.append(m)

    nc = build_nc()
    res = run_bass_kernel_spmd(nc, in_maps, core_ids=list(range(NCORES)))
    outp = np.concatenate([res.results[i]["o"] for i in range(NCORES)], axis=0)
    return outp[None, :, :].astype(np.float32)
